# version 24
# baseline (speedup 1.0000x reference)
import numpy as np
import ml_dtypes
from contextlib import ExitStack

import concourse.bass as bass
import concourse.mybir as mybir
from concourse.bass_utils import run_bass_kernel_spmd

F32 = mybir.dt.float32
BF16 = mybir.dt.bfloat16
AF = mybir.ActivationFunctionType
ALU = mybir.AluOpType
AX = mybir.AxisListType
BF = ml_dtypes.bfloat16

D = 2048
NTP = 18
NEG = -1.0e30
EPS = 1e-5
INV_D = 1.0 / 2048.0
QSCALE = float(128 ** -0.5)

ENGS = ("sync", "scalar", "vector", "tensor", "gpsimd")


class Sem:
    __slots__ = ("h", "n")

    def __init__(self, h):
        self.h = h
        self.n = 0


class Prog:

    def __init__(self, nc, selfsems=None):
        self.nc = nc
        self.selfsems = selfsems or {}
        self.reset()

    def reset(self):
        self.ops = {e: [] for e in ENGS}
        self.lastwait = {e: {} for e in ENGS}
        self.lastinc = {}

    def op(self, eng, fn, inc=None, inc_n=1):
        ss = self.selfsems.get(eng)
        if ss is not None:
            li = self.lastinc.get(eng)
            if li is not None:
                self.wait(eng, li[0], li[1])
            if inc is not None:
                self.ops[eng].append(("op", fn, (inc, inc_n), None))
                inc.n += inc_n
                self.lastinc[eng] = (inc, inc.n)
            else:
                self.ops[eng].append(("op", fn, (ss, 1), None))
                ss.n += 1
                self.lastinc[eng] = (ss, ss.n)
        else:
            self.ops[eng].append(("op", fn, (inc, inc_n), None))
            if inc is not None:
                inc.n += inc_n

    def wait(self, eng, sem, val=None):
        v = sem.n if val is None else val
        if v <= 0:
            return
        lw = self.lastwait[eng]
        if lw.get(id(sem), -1) >= v:
            return
        lw[id(sem)] = v
        self.ops[eng].append(("wait", sem, v, None))

    def flush(self):
        with self.nc.Block() as blk:
            for eng in ENGS:
                lst = self.ops[eng]
                if not lst:
                    continue

                def make(lst):
                    def run(e):
                        for kind, a, b, c in lst:
                            if kind == "wait":
                                e.wait_ge(a.h, b)
                            else:
                                ins = a(e)
                                inc, inc_n = b
                                if inc is not None:
                                    ins.then_inc(inc.h, inc_n)
                    return run

                getattr(blk, eng)(make(lst))
        self.reset()


def build_nc(cfg=None):
    cfg = cfg or {}
    stop_after = cfg.get("stop_after")
    nc = bass.Bass("TRN2", target_bir_lowering=False)

    dx = nc.dram_tensor("x", [NTP * 128, D], F32, kind="ExternalInput")
    dw1 = nc.dram_tensor("w1", [D, D], F32, kind="ExternalInput")
    dwq = nc.dram_tensor("wq", [D, D], F32, kind="ExternalInput")
    dwk = nc.dram_tensor("wk", [D, D], F32, kind="ExternalInput")
    dwv = nc.dram_tensor("wv", [D, D], F32, kind="ExternalInput")
    dwo = nc.dram_tensor("wo", [D, D], F32, kind="ExternalInput")
    dvf = nc.dram_tensor("vflat", [D, 512], F32, kind="ExternalInput")
    duf = nc.dram_tensor("uflat", [512, D], F32, kind="ExternalInput")
    dw2 = nc.dram_tensor("w2p", [128, 128], F32, kind="ExternalInput")
    db1 = nc.dram_tensor("b1t", [128, 16], F32, kind="ExternalInput")
    db2 = nc.dram_tensor("b2b", [128, 8], F32, kind="ExternalInput")
    dsel = nc.dram_tensor("selb", [8, 512], BF16, kind="ExternalInput")
    didb = nc.dram_tensor("d_idb", [128, 128], BF16, kind="ExternalInput")
    dmq = nc.dram_tensor("d_mq", [4, 128], BF16, kind="ExternalInput")
    dmk = nc.dram_tensor("mk3", [3 * 4, 384], BF16, kind="ExternalInput")
    dy = nc.dram_tensor("y", [16 * 128, D], F32, kind="ExternalOutput")
    dtap = None
    if stop_after:
        dtap = nc.dram_tensor("tap", [10 * 128, D], F32, kind="ExternalOutput")

    ctx = ExitStack()
    sb = lambda n, s, d: ctx.enter_context(nc.sbuf_tensor(n, s, d))
    psb = lambda n, s, d: ctx.enter_context(nc.psum_tensor(n, s, d))

    Xb = sb("Xb", [128, 10 * D], F32)
    XnT = sb("XnT", [128, 10 * D], BF16)
    Xn = [sb("Xn0", [128, D], BF16), sb("Xn1", [128, D], BF16)]
    wS = [sb("wSA", [128, 8192], BF16), sb("wSB", [128, 8192], BF16)]
    W2b = sb("W2b", [128, 128], BF16)
    b1T = sb("b1T", [128, 16], F32)
    B2b = sb("B2b", [128, 8], F32)
    Selb = sb("Selb", [8, 512], BF16)
    idb = sb("idb", [128, 128], BF16)
    Mqb = sb("Mqb", [68, 128], BF16)
    Mqs = [Mqb[0:4, :], Mqb[32:36, :], Mqb[64:68, :]]
    Mkb = sb("Mkb", [68, 384], BF16)
    Mk = [Mkb[0:4, :], Mkb[32:36, :], Mkb[64:68, :]]
    stats = sb("stats", [128, 6 * 10], F32)
    lgacc = sb("lgacc", [128, 8 * 10], F32)
    alphB = sb("alphB", [128, 8 * 10], BF16)
    asf = [sb("asf0", [128, 8], F32), sb("asf1", [128, 8], F32)]
    asc = sb("asc", [128, 4 * 10], F32)

    pg = [psb("pg0", [128, 512], F32), psb("pg1", [128, 512], F32)]
    pg2s = psb("pg2s", [128, 512], F32)

    sems = {}

    def S(name):
        if name not in sems:
            sems[name] = Sem(ctx.enter_context(nc.semaphore(name)))
        return sems[name]

    p = Prog(nc)
    if cfg.get("selfsem", True):
        p.selfsems = {"vector": S("vself"), "scalar": S("sself")}

    XS = lambda l: Xb[:, l * D:(l + 1) * D]
    XnTS = lambda l, c: XnT[:, l * D + c * 128: l * D + (c + 1) * 128]

    def xnt_rhs(l0, ntile, c):
        pstr = XnT.ap().ap[0][0]
        return bass.AP(XnT, l0 * D + c * 128, [[pstr, 128], [D, ntile], [1, 128]])

    def slot_dst(slot, nchunk, width, base=0):
        pstr = slot.ap().ap[0][0]
        return bass.AP(slot, base, [[pstr, 128], [width, nchunk], [1, width]])

    def wpiece_cols(dram, col0, width=512):
        ncols = dram.shape[1]
        return bass.AP(dram, col0, [[ncols, 128], [128 * ncols, 16], [1, width]])

    def wpiece_cols_h(dram, col0, h, width=512):
        ncols = dram.shape[1]
        return bass.AP(dram, col0 + h * 8 * 128 * ncols,
                       [[ncols, 128], [128 * ncols, 8], [1, width]])

    def wpiece_rows(dram, row0):
        return bass.AP(dram, row0 * D, [[D, 128], [128 * D, 4], [1, D]])

    def wpiece_rows_h(dram, row0):
        return bass.AP(dram, row0 * D, [[D, 128], [128 * D, 2], [1, D]])

    s_dx, s_dy, s_dc = S("dx"), S("dy"), S("dc")
    s_dwS = {"A": S("dwA"), "B": S("dwB"), "C": S("dwC")}
    dxv = {}
    wdn = {}

    def dma_w(tag, slot_key, dst_ap, src_ap):
        sem = s_dwS[slot_key]
        p.op("gpsimd", lambda e, dd=dst_ap, ss=src_ap: e.dma_start(dd, ss),
             inc=sem, inc_n=16)
        wdn[tag] = (sem, sem.n)

    def wait_w(eng, tag):
        sem, v = wdn[tag]
        p.wait(eng, sem, v)


    cv = {}

    def ph_consts():
        pairs = [("b1t", b1T[:, :], db1[:, :]), ("b2b", B2b[:, :], db2[:, :]),
                 ("selb", Selb[:, :], dsel[:, :]), ("idb", idb[:, :], didb[:, :]),
                 ("mq0", Mqs[0], dmq[:, :]), ("mq1", Mqs[1], dmq[:, :]),
                 ("mq2", Mqs[2], dmq[:, :]),
                 ("mk0", Mk[0], dmk[0:4, :]), ("mk1", Mk[1], dmk[4:8, :]),
                 ("mk2", Mk[2], dmk[8:12, :])]
        for name, dst, src in pairs:
            p.op("sync", lambda e, dst=dst, src=src: e.dma_start(
                dst, src), inc=s_dc, inc_n=16)
            cv[name] = s_dc.n
        dma_w(("w2p",), "C", bass.AP(W2b, 0, [[W2b.ap().ap[0][0], 128], [1, 128]]),
              dw2[:, :])

    def ph_loadx(c):
        if c == 1:
            p.wait("sync", s_dy, 16 * 8)
        for l in range(10):
            g = 8 * c + l
            sem = S(f"dx{l}")
            p.op("sync", lambda e, l=l, g=g: e.dma_start(
                XS(l), dx[g * 128:(g + 1) * 128, :]), inc=sem, inc_n=16)
            dxv[(c, l)] = (sem, sem.n)

    def ln_stats(l, sk, gates, mavd):
        s_st, s_ma = S("st"), S("ma")
        sl = lambda k: stats[:, l * 6 + k: l * 6 + k + 1]
        for (gs, gv) in gates:
            p.wait("scalar", gs, gv)
            p.wait("vector", gs, gv)
        p.op("scalar", lambda e, l=l, sk=sk, a=sl(0): e.activation(
            sk, XS(l), AF.Copy, accum_out=a), inc=s_st)
        p.wait("scalar", s_st)
        p.op("scalar", lambda e, l=l, sk=sk, a=sl(1): e.activation(
            sk, XS(l), AF.Square, accum_out=a), inc=s_st)
        p.wait("vector", s_st)
        p.op("vector", lambda e, a=sl(2), b=sl(0): e.tensor_scalar_mul(a, b, INV_D))
        p.op("vector", lambda e, a=sl(3), b=sl(1): e.tensor_scalar_mul(a, b, INV_D))
        p.op("vector", lambda e, a=sl(0), b=sl(2): e.tensor_mul(a, b, b))
        p.op("vector", lambda e, a=sl(3), b=sl(0): e.tensor_sub(a, a, b))
        p.op("vector", lambda e, a=sl(3): e.tensor_scalar_add(a, a, EPS), inc=s_ma)
        mavd[l] = s_ma.n

    def make_ln(c, pT, sink, wait_dx=False, premav=None):
        s_st, s_ma, s_rs, s_ap = S("st"), S("ma"), S("rs"), S("ap")
        s_tr, s_te = S("tr"), S("te")
        sl = lambda k, l: stats[:, l * 6 + k: l * 6 + k + 1]
        state = {"trb": [None, None], "te": [None] * len(pT), "pend": []}
        mav = dict(premav) if premav is not None else {}
        rsv = {}
        te = {}

        def stage1(l):
            gates = [dxv[(c, l)]] if wait_dx else []
            ln_stats(l, sink, gates, mav)

        def stage2(l):
            xn = Xn[l % 2]
            p.wait("scalar", s_ma, mav[l])
            p.op("scalar", lambda e, l=l: e.activation(
                sl(1, l), sl(3, l), AF.Sqrt), inc=s_rs)
            rsv[l] = s_rs.n
            p.wait("vector", s_rs, rsv[l])
            p.op("vector", lambda e, l=l: e.reciprocal(sl(4, l), sl(1, l)))
            if state["trb"][l % 2] is not None:
                p.wait("vector", s_tr, state["trb"][l % 2])
            p.op("vector", lambda e, l=l, xn=xn: e.tensor_scalar(
                xn[:, :], XS(l), sl(2, l), sl(4, l), ALU.subtract, ALU.mult),
                 inc=s_ap)
            p.wait("tensor", s_ap)
            for half in range(2):
                bi = half % len(pT)
                if state["te"][bi] is not None:
                    p.wait("tensor", s_te, state["te"][bi])
                for j in range(8):
                    cc = half * 8 + j
                    p.op("tensor", lambda e, xn=xn, cc=cc, bi=bi, j=j: e.transpose(
                        pT[bi][:, j * 128:(j + 1) * 128],
                        xn[:, cc * 128:(cc + 1) * 128], idb[:, :]),
                        inc=(s_tr if j == 7 else None))
                p.wait("vector", s_tr)
                p.op("vector", lambda e, l=l, half=half, bi=bi: e.tensor_copy(
                    XnT[:, l * D + half * 1024: l * D + (half + 1) * 1024],
                    pT[bi][:, :]), inc=s_te)
                state["te"][bi] = s_te.n
            state["trb"][l % 2] = s_tr.n
            te[l] = s_te.n

        def emit(l):
            if premav is not None:
                stage2(l)
                return
            if state["pend"]:
                stage2(state["pend"].pop(0))
            stage1(l)
            state["pend"].append(l)

        def drain():
            while state["pend"]:
                stage2(state["pend"].pop(0))

        emit.te = te
        emit.sem = s_te
        emit.drain = drain
        return emit

    def ph_ffn_w1(c, pid, wb, tiles, h_s, plg, ln=None):
        s_hmm, s_hev, s_lgm, s_lge = S("hmm"), S("hev"), S("lgm"), S("lge")
        ob = 4096 - wb
        n = len(tiles)
        l0 = tiles[0]
        tsubs = [(l0 + 4 * i, min(4, n - 4 * i)) for i in range((n + 3) // 4)]
        p.wait("vector", s_dc)
        for l in tiles:
            p.op("vector", lambda e, l=l: e.tensor_copy(
                lgacc[:, l * 8:(l + 1) * 8], B2b[:, :]))
        wait_w("tensor", ("w2p",))
        hev_bank = [None, None]
        hev_end = {}

        def hsl(g, s4, a, b):
            off = (g % 2) * 5120 + s4 * 1280
            return h_s[:, off + a: off + b]

        def emit_logits(g, i, l):
            p.wait("tensor", s_lge)
            p.wait("tensor", s_hev, hev_end[g])
            for s4 in range(2):
                p.op("tensor", lambda e, l=l, i=i, s4=s4, g=g: e.matmul(
                    plg[:, l * 8:(l + 1) * 8],
                    hsl(g, s4, i * 128, (i + 1) * 128),
                    W2b[:, (g * 2 + s4) * 8:(g * 2 + s4 + 1) * 8],
                    start=(s4 == 0), stop=(s4 == 1)),
                    inc=(s_lgm if s4 == 1 else None))
            p.wait("vector", s_lgm)
            p.op("vector", lambda e, l=l: e.tensor_add(
                lgacc[:, l * 8:(l + 1) * 8], lgacc[:, l * 8:(l + 1) * 8],
                plg[:, l * 8:(l + 1) * 8]), inc=s_lge)

        if ln is not None:
            for l in tiles[:4]:
                ln(l)
        lnq = list(tiles[4:]) if ln is not None else []
        for g in range(8):
            slot = wS[g % 2]
            if g < 2:
                tag = ("w1", c, pid, g)
                if tag not in wdn:
                    dma_w(tag, "AB"[g % 2], slot_dst(slot, 16, 256, wb),
                          wpiece_cols(dw1, g * 256, 256))
                if g == 1:
                    for h in range(2):
                        dma_w(("vfh", c, pid, h), "AB"[h],
                              slot_dst(wS[h], 8, 512, ob),
                              bass.AP(dvf, h * 8 * 128 * 512,
                                      [[512, 128], [128 * 512, 8], [1, 512]]))
            wait_w("tensor", ("w1", c, pid, g))
            if g == 0:
                pieces = [(s4, t0, nt) for (t0, nt) in tsubs for s4 in range(2)]
            else:
                pieces = [(s4, t0, nt) for s4 in range(2) for (t0, nt) in tsubs]
            for pi, (s4, t0, nt) in enumerate(pieces):
                bank = pg[pi % 2]
                if ln is not None:
                    while any(l2 not in ln.te for l2 in range(t0, t0 + nt)):
                        if lnq:
                            ln(lnq.pop(0))
                        else:
                            ln.drain()
                    p.wait("tensor", ln.sem, max(ln.te[l2]
                                                 for l2 in range(t0, t0 + nt)))
                if hev_bank[pi % 2] is not None:
                    p.wait("tensor", s_hev, hev_bank[pi % 2])
                for cc in range(16):
                    p.op("tensor", lambda e, bank=bank, slot=slot, cc=cc, s4=s4,
                         t0=t0, nt=nt: e.matmul(
                             bank[:, 0:nt * 128],
                             slot[:, wb + cc * 256 + s4 * 128:
                                  wb + cc * 256 + (s4 + 1) * 128],
                             xnt_rhs(t0, nt, cc),
                             start=(cc == 0), stop=(cc == 15)),
                         inc=(s_hmm if cc == 15 else None))
                if g > 0 and pi < n:
                    emit_logits(g - 1, pi, tiles[pi])
                if g == 0 and lnq:
                    ln(lnq.pop(0))
                p.wait("scalar", s_hmm)
                toff = (t0 - l0) * 128
                p.op("scalar", lambda e, bank=bank, s4=s4, toff=toff, nt=nt, g=g:
                     e.activation(hsl(g, s4, toff, toff + nt * 128),
                                  bank[:, 0:nt * 128], AF.Relu,
                                  bias=b1T[:, g * 2 + s4: g * 2 + s4 + 1]),
                     inc=s_hev)
                hev_bank[pi % 2] = s_hev.n
            if g > 0:
                for i2 in range(len(pieces), n):
                    emit_logits(g - 1, i2, tiles[i2])
            hev_end[g] = s_hev.n
            if g + 2 < 8:
                p.wait("gpsimd", s_hmm)
                dma_w(("w1", c, pid, g + 2), "AB"[g % 2],
                      slot_dst(wS[(g + 2) % 2], 16, 256, wb),
                      wpiece_cols(dw1, (g + 2) * 256, 256))
            if g >= 6:
                p.wait("gpsimd", s_hmm)
                h = g - 6
                dma_w(("ufh", c, pid, h), "AB"[h], slot_dst(wS[h], 2, D, wb),
                      bass.AP(duf, h * 256 * D, [[D, 128], [128 * D, 2], [1, D]]))
        return [(lambda i=i, l=l: emit_logits(7, i, l))
                for i, l in enumerate(tiles)]

    uxv = {}

    def ph_ffn_vu(c, pid, wb, tiles, z4, Ab4, aT, pAT, tail=None, prefetch=None,
                  stats_cb=None):
        ob = 4096 - wb
        s_alm, s_ale, s_alx = S("alm"), S("ale"), S("alx")
        s_vm, s_ve, s_um, s_ux = S("vm"), S("ve"), S("um"), S("ux")
        s_atm, s_ate, s_abm, s_abe = S("atm"), S("ate"), S("abm"), S("abe")
        s_hev = S("hev")
        n = len(tiles)
        l0 = tiles[0]
        tails = list(tail or [])
        alxv = {}
        alx_after = {}

        def router(i, l):
            par = i % 2
            sl = lambda k, l: asc[:, l * 4 + k: l * 4 + k + 1]
            p.op("vector", lambda e, l=l: e.reduce_max(
                sl(0, l), lgacc[:, l * 8:(l + 1) * 8], AX.X, negate=True), inc=s_alm)
            if alx_after.get(par) is not None:
                p.wait("scalar", s_alx, alx_after[par])
            p.wait("scalar", s_alm)
            p.op("scalar", lambda e, l=l, par=par: e.activation(
                asf[par][:, :], lgacc[:, l * 8:(l + 1) * 8], AF.Exp,
                bias=sl(0, l), accum_out=sl(1, l)), inc=s_ale)
            p.wait("vector", s_ale)
            p.op("vector", lambda e, l=l: e.reciprocal(sl(2, l), sl(1, l)))
            p.op("vector", lambda e, l=l, par=par: e.tensor_scalar_mul(
                alphB[:, l * 8:(l + 1) * 8], asf[par][:, :], sl(2, l)), inc=s_alx)
            alx_after[par] = s_alx.n
            alxv[l] = s_alx.n

        def alpha_sel(i0, nt):
            p.wait("tensor", s_alx, max(alxv[tiles[i0 + tt]] for tt in range(nt)))
            p.wait("tensor", s_ate)
            for tt in range(nt):
                l = tiles[i0 + tt]
                p.op("tensor", lambda e, l=l, tt=tt: e.transpose(
                    pAT[:, tt * 128:(tt + 1) * 128],
                    alphB[:, l * 8:(l + 1) * 8], idb[:, :]),
                    inc=(s_atm if tt == nt - 1 else None))
            p.wait("vector", s_atm)
            p.wait("vector", s_abm)
            p.op("vector", lambda e, nt=nt: e.tensor_copy(
                aT[:, 0:nt * 128], pAT[:, 0:nt * 128]), inc=s_ate)
            p.wait("tensor", s_ate)
            for kc2 in range(4):
                p.wait("tensor", s_abe)
                p.op("tensor", lambda e, kc2=kc2, nt=nt: e.matmul(
                    pg2s[:, 0:nt * 128],
                    Selb[:, kc2 * 128:(kc2 + 1) * 128], aT[:, 0:nt * 128],
                    start=True, stop=True), inc=s_abm)
                p.wait("vector", s_abm)
                p.op("vector", lambda e, kc2=kc2, nt=nt: e.tensor_copy(
                    Ab4[:, kc2 * 512: kc2 * 512 + nt * 128], pg2s[:, 0:nt * 128]),
                    inc=s_abe)
            return s_abe.n

        tsubs = [(4 * i, min(4, n - 4 * i)) for i in range((n + 3) // 4)]
        p.wait("tensor", s_dc)
        wait_w("tensor", ("vfh", c, pid, 0))
        wait_w("tensor", ("vfh", c, pid, 1))
        hev0 = s_hev.n
        ve_bank = [None, None]
        ux_bank = [None, None]
        pf = list(prefetch or [])
        for ti, (i0, nt) in enumerate(tsubs):
            first = (ti == 0)
            vmv = {}
            st = {"abe": None}

            def vx_mm(kc, splice=False, i0=i0, nt=nt, first=first, ti=ti,
                      vmv=vmv):
                bank = pg[kc % 2]
                p.wait("tensor", s_ux)
                if ve_bank[kc % 2] is not None:
                    p.wait("tensor", s_ve, ve_bank[kc % 2])
                if first and kc < 2:
                    p.wait("tensor", s_hev, hev0)
                for cc in range(16):
                    half, ci = cc // 8, cc % 8
                    p.op("tensor", lambda e, bank=bank, kc=kc, half=half, ci=ci,
                         cc=cc: e.matmul(
                             bank[:, 0:nt * 128],
                             wS[half][:, ob + ci * 512 + kc * 128:
                                      ob + ci * 512 + (kc + 1) * 128],
                             xnt_rhs(l0 + i0, nt, cc),
                             start=(cc == 0), stop=(cc == 15)),
                         inc=(s_vm if cc == 15 else None))
                    if splice and cc % 3 == 2 and tails:
                        tails.pop(0)()
                vmv[kc] = s_vm.n
                if ti == len(tsubs) - 1 and kc == 3 and pf:
                    p.wait("gpsimd", s_vm)
                    while pf:
                        dma_w(*pf.pop(0))

            def z_evac(kc, nt=nt, vmv=vmv, st=st):
                bank = pg[kc % 2]
                p.wait("vector", s_vm, vmv[kc])
                p.wait("vector", s_abe, st["abe"])
                p.wait("vector", s_um)
                p.op("vector", lambda e, bank=bank, kc=kc: e.scalar_tensor_tensor(
                    z4[:, kc * 512: kc * 512 + nt * 128], bank[:, 0:nt * 128], 0.5,
                    Ab4[:, kc * 512: kc * 512 + nt * 128], ALU.mult, ALU.mult),
                    inc=s_ve)
                ve_bank[kc % 2] = s_ve.n

            vx_mm(0, splice=first)
            if first:
                vx_mm(1, splice=True)
                while tails:
                    tails.pop(0)()
                for i, l in enumerate(tiles):
                    router(i, l)
                st["abe"] = alpha_sel(i0, nt)
            else:
                st["abe"] = alpha_sel(i0, nt)
                vx_mm(1)
            z_evac(0)
            z_evac(1)
            vx_mm(2)
            z_evac(2)
            vx_mm(3)
            z_evac(3)
            wait_w("tensor", ("ufh", c, pid, 0))
            wait_w("tensor", ("ufh", c, pid, 1))
            p.wait("tensor", s_ve)
            for tt in range(nt):
                l = tiles[i0 + tt]
                for cg in range(4):
                    bank = pg[cg % 2]
                    if ux_bank[cg % 2] is not None:
                        p.wait("tensor", s_ux, ux_bank[cg % 2])
                    for kc in range(4):
                        p.op("tensor", lambda e, bank=bank, tt=tt, cg=cg, kc=kc:
                             e.matmul(bank[:, :],
                                      z4[:, kc * 512 + tt * 128: kc * 512 + (tt + 1) * 128],
                                      wS[kc // 2][:, wb + (kc % 2) * D + cg * 512:
                                                  wb + (kc % 2) * D + (cg + 1) * 512],
                                      start=(kc == 0), stop=(kc == 3)),
                             inc=(s_um if kc == 3 else None))
                    p.wait("vector", s_um)
                    p.op("vector", lambda e, bank=bank, l=l, cg=cg: e.tensor_add(
                        Xb[:, l * D + cg * 512: l * D + (cg + 1) * 512],
                        Xb[:, l * D + cg * 512: l * D + (cg + 1) * 512],
                        bank[:, :]), inc=s_ux)
                    ux_bank[cg % 2] = s_ux.n
                uxv[l] = s_ux.n
                if stats_cb is not None:
                    stats_cb(l)

    def ph_attn(c, hg, A, ln=None, next_a=None, stats_cb=None):
        (Kb, Vt, Qb, Praw, Pn, PTs, aoAll, att4, pS, pPT) = A
        s_pm, s_peA, s_peV = S("pm"), S("peA"), S("peV")
        s_sm, s_nm, s_px, s_pn, s_ptm, s_ptd, s_aom, s_aoev, s_wom, s_wx = (
            S("sm"), S("nm"), S("px"), S("pn"), S("ptm"), S("ptd"), S("aom"),
            S("aoev"), S("wom"), S("wx"))
        tsubs = [(0, 4), (4, 4), (8, 2)]
        tiles10 = list(range(10))
        for h in range(2):
            tag = ("wk", c, hg, h)
            if tag not in wdn:
                dma_w(tag, "AB"[h], slot_dst(wS[h], 8, 512, 4096),
                      wpiece_cols_h(dwk, hg * 512, h))
        for h in range(2):
            dma_w(("wq", c, hg, h), "AB"[h], slot_dst(wS[h], 8, 512, 0),
                  wpiece_cols_h(dwq, hg * 512, h))
        p.wait("tensor", s_dc)

        pm_bank = [None, None]
        pmix = [0]
        kpe = {}
        vpe = {}
        qpe = {}

        def proj_piece(kind, *args):
            b = pmix[0] % 2
            pmix[0] += 1
            bank = pg[b]
            if pm_bank[b] is not None:
                p.wait("tensor", pm_bank[b][0], pm_bank[b][1])
            if kind == "k":
                h, (i0, nt) = args
                wait_w("tensor", ("wk", c, hg, 0))
                wait_w("tensor", ("wk", c, hg, 1))
                for cc in range(16):
                    p.op("tensor", lambda e, bank=bank, h=h, cc=cc, i0=i0, nt=nt:
                         e.matmul(bank[:, 0:nt * 128],
                                  wS[cc // 8][:, 4096 + (cc % 8) * 512 + h * 128:
                                              4096 + (cc % 8) * 512 + (h + 1) * 128],
                                  xnt_rhs(i0, nt, cc),
                                  start=(cc == 0), stop=(cc == 15)),
                         inc=(s_pm if cc == 15 else None))
                p.wait("scalar", s_pm)
                p.op("scalar", lambda e, bank=bank, h=h, i0=i0, nt=nt: e.copy(
                    Kb[:, h * 1280 + i0 * 128: h * 1280 + (i0 + nt) * 128],
                    bank[:, 0:nt * 128]), inc=s_peA)
                kpe[(h, i0)] = s_peA.n
                pm_bank[b] = (s_peA, s_peA.n)
            elif kind == "v":
                l, = args
                wait_w("tensor", ("wv", c, hg, 0))
                wait_w("tensor", ("wv", c, hg, 1))
                for cc in range(16):
                    p.op("tensor", lambda e, bank=bank, l=l, cc=cc: e.matmul(
                        bank[:, :], XnTS(l, cc),
                        wS[cc // 8][:, 4096 + (cc % 8) * 512:
                                    4096 + (cc % 8 + 1) * 512],
                        start=(cc == 0), stop=(cc == 15)),
                        inc=(s_pm if cc == 15 else None))
                p.wait("scalar", s_pm)
                p.op("scalar", lambda e, bank=bank, l=l: e.copy(
                    Vt[:, l * 512:(l + 1) * 512], bank[:, :]), inc=s_peA)
                vpe[l] = s_peA.n
                pm_bank[b] = (s_peA, s_peA.n)
            else:
                h, (i0, nt) = args
                wait_w("tensor", ("wq", c, hg, 0))
                wait_w("tensor", ("wq", c, hg, 1))
                for cc in range(16):
                    p.op("tensor", lambda e, bank=bank, h=h, cc=cc, i0=i0, nt=nt:
                         e.matmul(bank[:, 0:nt * 128],
                                  wS[cc // 8][:, (cc % 8) * 512 + h * 128:
                                              (cc % 8) * 512 + (h + 1) * 128],
                                  xnt_rhs(i0, nt, cc),
                                  start=(cc == 0), stop=(cc == 15)),
                         inc=(s_pm if cc == 15 else None))
                p.wait("vector", s_pm)
                p.op("vector", lambda e, bank=bank, h=h, i0=i0, nt=nt:
                     e.tensor_scalar_mul(
                         Qb[:, h * 1024 + (i0 - 1) * 128: h * 1024 + (i0 - 1 + nt) * 128],
                         bank[:, 0:nt * 128], QSCALE), inc=s_peV)
                qpe[(h, i0)] = s_peV.n
                pm_bank[b] = (s_peV, s_peV.n)

        at = lambda k, par: att4[:, par * 8 + k: par * 8 + k + 1]
        at2 = lambda k, par: att4[:, par * 8 + k: par * 8 + k + 2]
        qtiles = list(cfg.get("qtiles", range(1, 9)))
        its = [(q, hp) for q in qtiles for hp in range(2)]
        n_it = len(its)
        smv = {}; nmv = {}; pxv = {}; pnv = {}; ptmv = {}; ptdv = {}
        aomv = {}; aoevv = {}
        wo_read = {}

        def st_a(i):
            q, hp = its[i]
            par = i % 2
            bank = pS[par]
            gq = 8 * c + q
            mi = 0 if gq == 1 else (2 if gq == 16 else 1)
            mkt, mqt = Mk[mi], Mqs[mi]
            qts = 1 if q <= 4 else 5
            p.wait("tensor", s_peV, qpe[(2 * hp + 1, qts)])
            p.wait("tensor", s_peA, kpe[(2 * hp + 1, 8)])
            if i - 2 >= 0:
                p.wait("tensor", s_px, pxv[i - 2])
            for hh in range(2):
                h = 2 * hp + hh
                p.op("tensor", lambda e, bank=bank, mkt=mkt, mqt=mqt, hh=hh:
                     e.matmul(bank[:, hh * 512: hh * 512 + 384], mqt, mkt,
                              start=True, stop=False))
                p.op("tensor", lambda e, bank=bank, h=h, q=q, hh=hh: e.matmul(
                    bank[:, hh * 512: hh * 512 + 384],
                    Qb[:, h * 1024 + (q - 1) * 128: h * 1024 + q * 128],
                    Kb[:, h * 1280 + (q - 1) * 128: h * 1280 + (q - 1) * 128 + 384],
                    start=False, stop=True), inc=(s_sm if hh == 1 else None))
            smv[i] = s_sm.n

        def st_b(i):
            par = i % 2
            bank = pS[par]
            p.wait("vector", s_sm, smv[i])
            if i - 2 >= 0:
                p.wait("vector", s_px, pxv[i - 2])
            p.op("vector", lambda e, bank=bank, par=par: e.reduce_max(
                at(0, par),
                bass.AP(bank.tensor if hasattr(bank, "tensor") else bank, 0,
                        [[bank.ap().ap[0][0], 128], [512, 2], [1, 384]]),
                AX.XY, negate=True), inc=s_nm)
            nmv[i] = s_nm.n
            p.wait("scalar", s_nm, nmv[i])
            if i - 2 >= 0:
                p.wait("scalar", s_pn, pnv[i - 2])
            for hh in range(2):
                p.op("scalar", lambda e, bank=bank, par=par, hh=hh: e.activation(
                    Praw[par][:, hh * 384:(hh + 1) * 384],
                    bank[:, hh * 512: hh * 512 + 384], AF.Exp, bias=at(0, par),
                    accum_out=at(1 + hh, par)), inc=(s_px if hh == 1 else None))
            pxv[i] = s_px.n
            p.wait("vector", s_px, pxv[i])
            p.op("vector", lambda e, par=par: e.reciprocal(at2(3, par), at2(1, par)))
            if i - 2 >= 0:
                p.wait("vector", s_ptm, ptmv[i - 2])
            for hh in range(2):
                p.op("vector", lambda e, par=par, hh=hh: e.tensor_scalar_mul(
                    Pn[par][:, hh * 384:(hh + 1) * 384],
                    Praw[par][:, hh * 384:(hh + 1) * 384], at(3 + hh, par)),
                    inc=(s_pn if hh == 1 else None))
            pnv[i] = s_pn.n

        def st_c1(i):
            q, hp = its[i]
            par = i % 2
            p.wait("tensor", s_pn, pnv[i])
            if i - 1 >= 0:
                p.wait("tensor", s_ptd, ptdv[i - 1])
            elif ln is not None:
                p.wait("tensor", ln.sem)
            for kt in range(6):
                p.op("tensor", lambda e, par=par, kt=kt: e.transpose(
                    pPT[:, kt * 128:(kt + 1) * 128],
                    Pn[par][:, kt * 128:(kt + 1) * 128], idb[:, :]),
                    inc=(s_ptm if kt == 5 else None))
            ptmv[i] = s_ptm.n
            eng = "scalar" if i % 2 else "vector"
            p.wait(eng, s_ptm, ptmv[i])
            if i - 2 >= 0:
                p.wait(eng, s_aom, aomv[i - 2])
            if eng == "scalar":
                p.op(eng, lambda e, par=par: e.copy(PTs[par][:, :], pPT[:, 0:768]),
                     inc=s_ptd)
            else:
                p.op(eng, lambda e, par=par: e.tensor_copy(PTs[par][:, :], pPT[:, 0:768]),
                     inc=s_ptd)
            ptdv[i] = s_ptd.n

        def st_c2(i):
            q, hp = its[i]
            par = i % 2
            qi = qtiles.index(q)
            p.wait("tensor", s_ptd, ptdv[i])
            p.wait("tensor", s_peA, vpe[min(q + 1, 9)])
            if hp == 0 and qi - 1 >= 0:
                p.wait("tensor", s_aoev, aoevv[qtiles[qi - 1]])
            for j in range(6):
                hh, kt = j // 3, j % 3
                h = 2 * hp + hh
                p.op("tensor", lambda e, par=par, h=h, hh=hh, q=q, kt=kt: e.matmul(
                    pg2s[:, h * 128:(h + 1) * 128],
                    Vt[:, (q - 1 + kt) * 512 + h * 128:
                       (q - 1 + kt) * 512 + (h + 1) * 128],
                    pPT[:, :] if False else PTs[par][:, (hh * 3 + kt) * 128:(hh * 3 + kt + 1) * 128],
                    start=(kt == 0), stop=(kt == 2)),
                    inc=(s_aom if j == 5 else None))
            aomv[i] = s_aom.n

        def st_d(i):
            q, hp = its[i]
            qi = qtiles.index(q)
            slot = qi % 4
            p.wait("scalar", s_aom, aomv[i])
            if wo_read.get(slot) is not None:
                p.wait("scalar", s_wom, wo_read[slot])
            p.op("scalar", lambda e, slot=slot: e.copy(
                aoAll[:, slot * 512:(slot + 1) * 512], pg2s[:, :]), inc=s_aoev)
            aoevv[q] = s_aoev.n

        wx_bank = [None, None]

        def wo_section(qs):
            p.wait("tensor", s_peA)
            p.wait("tensor", s_peV)
            for q in qs:
                qi = qtiles.index(q)
                slot = qi % 4
                p.wait("tensor", s_aoev, aoevv[q])
                for cg in range(4):
                    bsel = cg % 2
                    wbank = pg[bsel]
                    if wx_bank[bsel] is not None:
                        p.wait("tensor", s_wx, wx_bank[bsel])
                    for h2 in range(4):
                        p.op("tensor", lambda e, wbank=wbank, slot=slot, h2=h2, cg=cg:
                             e.matmul(wbank[:, :],
                                      aoAll[:, slot * 512 + h2 * 128: slot * 512 + (h2 + 1) * 128],
                                      wS[h2 // 2][:, (h2 % 2) * D + cg * 512:
                                                  (h2 % 2) * D + (cg + 1) * 512],
                                      start=(h2 == 0), stop=(h2 == 3)),
                             inc=(s_wom if h2 == 3 else None))
                    wv = s_wom.n
                    p.wait("vector", s_wom, wv)
                    p.op("vector", lambda e, wbank=wbank, q=q, cg=cg: e.tensor_add(
                        Xb[:, q * D + cg * 512: q * D + (cg + 1) * 512],
                        Xb[:, q * D + cg * 512: q * D + (cg + 1) * 512],
                        wbank[:, :]), inc=s_wx)
                    wx_bank[bsel] = s_wx.n
                wo_read[slot] = s_wom.n
                if stats_cb is not None:
                    stats_cb(q, (s_wx, s_wx.n))

        def ln_wait(t0, nt):
            if ln is not None:
                if any(l2 not in ln.te for l2 in range(t0, t0 + nt)):
                    ln.drain()
                p.wait("tensor", ln.sem, max(ln.te[l2] for l2 in range(t0, t0 + nt)))

        if ln is not None:
            for l in tiles10[:4]:
                if l not in ln.te:
                    ln(l)
        lnq = [l for l in tiles10[4:] if l not in ln.te] if ln is not None else []
        korder = [("k", h, ts) for ts in tsubs for h in range(4)]
        for (kind, h, ts) in korder:
            ln_wait(*ts)
            proj_piece("k", h, ts)
            if lnq:
                ln(lnq.pop(0))
        pmK = s_pm.n
        p.wait("gpsimd", s_pm, pmK)
        for h in range(2):
            dma_w(("wv", c, hg, h), "AB"[h], slot_dst(wS[h], 8, 512, 4096),
                  wpiece_cols_h(dwv, hg * 512, h))
        if ln is not None:
            ln.drain()
            p.wait("tensor", ln.sem)
        for h in range(4):
            proj_piece("q", h, (1, 4))
            proj_piece("q", h, (5, 4))
        vq = list(range(10))
        for l in range(3):
            proj_piece("v", vq.pop(0))
        p.wait("gpsimd", s_pm)
        for h in range(2):
            dma_w(("wo", c, hg, h), "AB"[h], slot_dst(wS[h], 2, D, 0),
                  wpiece_rows_h(dwo, hg * 512 + h * 256))
        for i in range(n_it + 3):
            if i < n_it:
                st_a(i)
            if vq:
                proj_piece("v", vq.pop(0))
                if not vq and next_a is not None:
                    p.wait("gpsimd", s_pm)
                    for spec in next_a:
                        dma_w(*spec)
            if 0 <= i - 1 < n_it:
                st_b(i - 1)
            if 0 <= i - 2 < n_it:
                st_c1(i - 2)
            if 0 <= i - 3 < n_it:
                st_c2(i - 3)
                if its[i - 3][1] == 1:
                    st_d(i - 3)
                    qdone = qtiles.index(its[i - 3][0])
                    if qdone % 2 == 1:
                        wait_w("tensor", ("wo", c, hg, 0))
                        wait_w("tensor", ("wo", c, hg, 1))
                        wo_section(qtiles[qdone - 1:qdone + 1])
        if len(qtiles) % 2 == 1:
            wait_w("tensor", ("wo", c, hg, 0))
            wait_w("tensor", ("wo", c, hg, 1))
            wo_section(qtiles[-1:])

    def ph_store(c, load_next=False, stats_cb=None):
        s_ux = S("ux")
        dy0 = s_dy.n
        for q in range(1, 9):
            g = 8 * c + q
            p.wait("sync", s_ux, uxv[q])
            p.op("sync", lambda e, q=q, g=g: e.dma_start(
                dy[(g - 1) * 128:g * 128, :], XS(q)), inc=s_dy, inc_n=16)
        if load_next:
            for l in [0, 9] + list(range(1, 9)):
                g = 8 * (c + 1) + l
                if 1 <= l <= 8:
                    p.wait("sync", s_dy, dy0 + 16 * l)
                sem = S(f"dx{l}")
                p.op("sync", lambda e, l=l, g=g: e.dma_start(
                    XS(l), dx[g * 128:(g + 1) * 128, :]), inc=sem, inc_n=16)
                dxv[(c + 1, l)] = (sem, sem.n)
                if stats_cb is not None:
                    stats_cb(l)

    def ph_tap(tiles):
        p.wait("sync", S("ux"))
        p.wait("sync", S("wx"))
        for l in tiles:
            p.op("sync", lambda e, l=l: e.dma_start(
                dtap[l * 128:(l + 1) * 128, :], XS(l)), inc=s_dy, inc_n=16)
        p.wait("sync", s_dy)

    nchunks = 1 if stop_after else 2
    uid = [0]

    def un(base):
        uid[0] += 1
        return f"{base}_{uid[0]}"

    ph_consts()
    ln1mav = {}
    for c in range(nchunks):
        ln2mav = {}
        ln3mav = {}
        if (c, 0) not in dxv:
            ph_loadx(c)
        with ExitStack() as fs:
            h_s = fs.enter_context(nc.sbuf_tensor(un("h_s"), [128, 2 * 4 * 1280], BF16))
            z4 = fs.enter_context(nc.sbuf_tensor(un("z4"), [128, 4 * 512], BF16))
            Ab4 = fs.enter_context(nc.sbuf_tensor(un("Ab4"), [128, 4 * 512], BF16))
            aT = fs.enter_context(nc.sbuf_tensor(un("aT"), [8, 512], BF16))
            plg = fs.enter_context(nc.psum_tensor(un("plg"), [128, 80], F32))
            pT0 = fs.enter_context(nc.psum_tensor(un("pT0"), [128, 1024], BF16))
            pT1 = fs.enter_context(nc.psum_tensor(un("pT1"), [128, 1024], BF16))
            ln1 = make_ln(c, [pT0, pT1], h_s[:, 2560:4608], wait_dx=True,
                          premav=(ln1mav if c > 0 else None))
            tail = ph_ffn_w1(c, 1, 0, list(range(10)), h_s, plg, ln=ln1)
            with nc.psum_tensor(un("pAT"), [8, 512], BF16) as pAT:
                pfx = None
                sc2 = None
                if not stop_after:
                    pfx = [(("wk", c, 0, h), "AB"[h],
                            slot_dst(wS[h], 8, 512, 4096),
                            wpiece_cols_h(dwk, 0, h)) for h in range(2)]
                    sc2 = (lambda l, hs=h_s: ln_stats(
                        l, hs[:, 2560:4608], [(S("ux"), uxv[l])], ln2mav))
                ph_ffn_vu(c, 1, 0, list(range(10)), z4, Ab4, aT, pAT, tail=tail,
                          prefetch=pfx, stats_cb=sc2)
                p.flush()
        if stop_after == "s1":
            ph_tap(list(range(10)))
            p.flush()
            break
        with ExitStack() as asx:
            Kb = asx.enter_context(nc.sbuf_tensor(un("Kb"), [128, 4 * 1280], BF16))
            Vt = asx.enter_context(nc.sbuf_tensor(un("Vt"), [128, 10 * 512], BF16))
            Qb = asx.enter_context(nc.sbuf_tensor(un("Qb"), [128, 4 * 1024], BF16))
            Praw = [asx.enter_context(nc.sbuf_tensor(un("Praw"), [128, 768], F32))
                    for _ in range(2)]
            Pn = [asx.enter_context(nc.sbuf_tensor(un("Pn"), [128, 768], BF16))
                  for _ in range(2)]
            PTs = [asx.enter_context(nc.sbuf_tensor(un("PTs"), [128, 768], BF16))
                   for _ in range(2)]
            aoAll = asx.enter_context(nc.sbuf_tensor(un("aoAll"), [128, 4 * 512], BF16))
            att4 = asx.enter_context(nc.sbuf_tensor(un("att4"), [128, 16], F32))
            pS = [asx.enter_context(nc.psum_tensor(un("pS"), [128, 1024], F32))
                  for _ in range(2)]
            pPT = asx.enter_context(nc.psum_tensor(un("pPT"), [128, 1024], BF16))
            A = (Kb, Vt, Qb, Praw, Pn, PTs, aoAll, att4, pS, pPT)
            ln2 = make_ln(c, [pPT], Qb[:, 0:2048],
                          premav=(ln2mav if not stop_after else None))
            for hg in range(cfg.get("hgs", 4)):
                scb = None
                if hg == 3 and not stop_after:
                    na = [(("w1", c, 3, g), "AB"[g], slot_dst(wS[g], 16, 256, 4096),
                           wpiece_cols(dw1, g * 256, 256)) for g in range(2)]
                    scb = (lambda q, gate: ln_stats(q, XS(9), [gate], ln3mav))
                elif hg + 1 < cfg.get("hgs", 4):
                    na = [(("wk", c, hg + 1, h), "AB"[h],
                           slot_dst(wS[h], 8, 512, 4096),
                           wpiece_cols_h(dwk, (hg + 1) * 512, h)) for h in range(2)]
                else:
                    na = None
                ph_attn(c, hg, A, ln=(ln2 if hg == 0 else None), next_a=na,
                        stats_cb=scb)
                p.flush()
        if stop_after == "att":
            ph_tap(list(range(10)))
            p.flush()
            break
        with ExitStack() as fs:
            h_s = fs.enter_context(nc.sbuf_tensor(un("h_s"), [128, 2 * 4 * 1280], BF16))
            z4 = fs.enter_context(nc.sbuf_tensor(un("z4"), [128, 4 * 512], BF16))
            Ab4 = fs.enter_context(nc.sbuf_tensor(un("Ab4"), [128, 4 * 512], BF16))
            aT = fs.enter_context(nc.sbuf_tensor(un("aT"), [8, 512], BF16))
            plg = fs.enter_context(nc.psum_tensor(un("plg"), [128, 80], F32))
            pT0 = fs.enter_context(nc.psum_tensor(un("pT0"), [128, 1024], BF16))
            pT1 = fs.enter_context(nc.psum_tensor(un("pT1"), [128, 1024], BF16))
            ln3 = make_ln(c, [pT0, pT1], h_s[:, 2560:4608], premav=ln3mav)
            tail = ph_ffn_w1(c, 3, 4096, list(range(1, 9)), h_s, plg, ln=ln3)
            with nc.psum_tensor(un("pAT"), [8, 512], BF16) as pAT:
                pfx = None
                sc1 = None
                if c + 1 < nchunks:
                    pfx = [(("w1", c + 1, 1, g), "AB"[g],
                            slot_dst(wS[g], 16, 256, 0),
                            wpiece_cols(dw1, g * 256, 256)) for g in range(2)]
                    sc1 = (lambda l, hs=h_s: ln_stats(
                        l, hs[:, 2560:4608], [dxv[(c + 1, l)]], ln1mav))
                ph_ffn_vu(c, 3, 4096, list(range(1, 9)), z4, Ab4, aT, pAT,
                          tail=tail, prefetch=pfx)
                ph_store(c, load_next=(c + 1 < nchunks), stats_cb=sc1)
                p.flush()
    if not stop_after:
        p.wait("sync", s_dy, 16 * 16)
        p.flush()

    ctx.close()
    return nc



def _build_masks():
    NEGf = np.float32(NEG)
    mq = np.zeros((4, 128), dtype=BF)
    mq[0, :64] = 1
    mq[1, 64:] = 1
    mq[2, :] = 1

    def mk(pad_lo=False, pad_hi=False):
        m = np.zeros((4, 384), dtype=np.float32)
        m[0, 320:] = NEGf
        m[1, :64] = NEGf
        if pad_lo:
            m[2, :128] = NEGf
        if pad_hi:
            m[2, 256:] = NEGf
        return m.astype(BF)

    return mq, mk(), mk(pad_lo=True), mk(pad_hi=True)


def _host_inputs(L0_attn_hidden_states, L0_attn_Wq, L0_attn_Wk, L0_attn_Wv,
                 L0_attn_Wo, L0_ffn_U, L0_ffn_V, L0_ffn_W1, L0_ffn_b1,
                 L0_ffn_W2, L0_ffn_b2):
    x = np.asarray(L0_attn_hidden_states, dtype=np.float32)
    w1 = np.ascontiguousarray(np.asarray(L0_ffn_W1, dtype=np.float32))
    wq = np.ascontiguousarray(np.asarray(L0_attn_Wq, dtype=np.float32))
    wk = np.ascontiguousarray(np.asarray(L0_attn_Wk, dtype=np.float32))
    wv = np.ascontiguousarray(np.asarray(L0_attn_Wv, dtype=np.float32))
    wo = np.ascontiguousarray(np.asarray(L0_attn_Wo, dtype=np.float32))
    U = np.asarray(L0_ffn_U, dtype=np.float32)
    V = np.asarray(L0_ffn_V, dtype=np.float32)
    vflat = np.ascontiguousarray(V.transpose(2, 0, 1).reshape(D, 512))
    uflat = np.ascontiguousarray(U.transpose(0, 2, 1).reshape(512, D))
    W2 = np.asarray(L0_ffn_W2, dtype=np.float32)
    w2p = np.ascontiguousarray(
        W2.reshape(16, 128, 8).transpose(1, 0, 2).reshape(128, 128))
    b1t = np.ascontiguousarray(np.asarray(L0_ffn_b1, np.float32).reshape(16, 128).T)
    b2b = np.ascontiguousarray(
        np.broadcast_to(np.asarray(L0_ffn_b2, np.float32), (128, 8)))
    sel = np.zeros((8, 512), dtype=BF)
    for kr in range(512):
        sel[kr // 64, kr] = 1
    idb = np.eye(128, dtype=BF)
    mq, mk_int, mk_lo, mk_hi = _build_masks()

    shared = dict(w1=w1, wq=wq, wk=wk, wv=wv, wo=wo, vflat=vflat, uflat=uflat,
                  w2p=w2p, b1t=b1t, b2b=b2b, selb=sel, d_idb=idb, d_mq=mq)
    in_maps = []
    for core in range(8):
        b, half = core // 2, core % 2
        xc = np.zeros((NTP * 128, D), dtype=np.float32)
        if half == 0:
            xc[128:] = x[b, 0:2176]
            mk3 = np.concatenate([mk_lo, mk_int, mk_int], axis=0)
        else:
            xc[0:2176] = x[b, 1920:4096]
            mk3 = np.concatenate([mk_int, mk_int, mk_hi], axis=0)
        m = dict(shared)
        m["x"] = xc
        m["mk3"] = np.ascontiguousarray(mk3)
        in_maps.append(m)
    return in_maps


_NC_CACHE = {}


def kernel(**inputs):
    in_maps = _host_inputs(**inputs)
    if "nc" not in _NC_CACHE:
        _NC_CACHE["nc"] = build_nc()
    nc = _NC_CACHE["nc"]
    res = run_bass_kernel_spmd(nc, in_maps, core_ids=list(range(8)))
    y = np.empty((4, 4096, D), dtype=np.float32)
    for core in range(8):
        b, half = core // 2, core % 2
        y[b, half * 2048:(half + 1) * 2048] = res.results[core]["y"]
    return y


# version 26
# speedup vs baseline: 1.0425x; 1.0425x over previous
import numpy as np
import ml_dtypes
from contextlib import ExitStack

import concourse.bass as bass
import concourse.mybir as mybir
from concourse.bass_utils import run_bass_kernel_spmd

F32 = mybir.dt.float32
BF16 = mybir.dt.bfloat16
AF = mybir.ActivationFunctionType
ALU = mybir.AluOpType
AX = mybir.AxisListType
BF = ml_dtypes.bfloat16

D = 2048
NTP = 18
NEG = -1.0e30
EPS = 1e-5
INV_D = 1.0 / 2048.0
QSCALE = float(128 ** -0.5)

ENGS = ("sync", "scalar", "vector", "tensor", "gpsimd")


class Sem:
    __slots__ = ("h", "n")

    def __init__(self, h):
        self.h = h
        self.n = 0


class Prog:

    def __init__(self, nc, selfsems=None):
        self.nc = nc
        self.selfsems = selfsems or {}
        self.reset()

    def reset(self):
        self.ops = {e: [] for e in ENGS}
        self.lastwait = {e: {} for e in ENGS}
        self.lastinc = {}

    def op(self, eng, fn, inc=None, inc_n=1):
        ss = self.selfsems.get(eng)
        if ss is not None:
            li = self.lastinc.get(eng)
            if li is not None:
                self.wait(eng, li[0], li[1])
            if inc is not None:
                self.ops[eng].append(("op", fn, (inc, inc_n), None))
                inc.n += inc_n
                self.lastinc[eng] = (inc, inc.n)
            else:
                self.ops[eng].append(("op", fn, (ss, 1), None))
                ss.n += 1
                self.lastinc[eng] = (ss, ss.n)
        else:
            self.ops[eng].append(("op", fn, (inc, inc_n), None))
            if inc is not None:
                inc.n += inc_n

    def wait(self, eng, sem, val=None):
        v = sem.n if val is None else val
        if v <= 0:
            return
        lw = self.lastwait[eng]
        if lw.get(id(sem), -1) >= v:
            return
        lw[id(sem)] = v
        self.ops[eng].append(("wait", sem, v, None))

    def flush(self):
        with self.nc.Block() as blk:
            for eng in ENGS:
                lst = self.ops[eng]
                if not lst:
                    continue

                def make(lst):
                    def run(e):
                        for kind, a, b, c in lst:
                            if kind == "wait":
                                e.wait_ge(a.h, b)
                            else:
                                ins = a(e)
                                inc, inc_n = b
                                if inc is not None:
                                    ins.then_inc(inc.h, inc_n)
                    return run

                getattr(blk, eng)(make(lst))
        self.reset()


def build_nc(cfg=None):
    cfg = cfg or {}
    stop_after = cfg.get("stop_after")
    nc = bass.Bass("TRN2", target_bir_lowering=False)

    dx = nc.dram_tensor("x", [NTP * 128, D], F32, kind="ExternalInput")
    dw1 = nc.dram_tensor("w1", [D, D], F32, kind="ExternalInput")
    dwq = nc.dram_tensor("wq", [D, D], F32, kind="ExternalInput")
    dwk = nc.dram_tensor("wk", [D, D], F32, kind="ExternalInput")
    dwv = nc.dram_tensor("wv", [D, D], F32, kind="ExternalInput")
    dwo = nc.dram_tensor("wo", [D, D], F32, kind="ExternalInput")
    dvf = nc.dram_tensor("vflat", [D, 512], F32, kind="ExternalInput")
    duf = nc.dram_tensor("uflat", [512, D], F32, kind="ExternalInput")
    dw2 = nc.dram_tensor("w2p", [128, 128], F32, kind="ExternalInput")
    db1 = nc.dram_tensor("b1t", [128, 16], F32, kind="ExternalInput")
    db2 = nc.dram_tensor("b2b", [128, 8], F32, kind="ExternalInput")
    dsel = nc.dram_tensor("selb", [8, 512], BF16, kind="ExternalInput")
    didb = nc.dram_tensor("d_idb", [128, 128], BF16, kind="ExternalInput")
    dmq = nc.dram_tensor("d_mq", [4, 128], BF16, kind="ExternalInput")
    dmk = nc.dram_tensor("mk3", [3 * 4, 384], BF16, kind="ExternalInput")
    dy = nc.dram_tensor("y", [16 * 128, D], F32, kind="ExternalOutput")
    dtap = None
    if stop_after:
        dtap = nc.dram_tensor("tap", [10 * 128, D], F32, kind="ExternalOutput")

    ctx = ExitStack()
    sb = lambda n, s, d: ctx.enter_context(nc.sbuf_tensor(n, s, d))
    psb = lambda n, s, d: ctx.enter_context(nc.psum_tensor(n, s, d))

    Xb = sb("Xb", [128, 10 * D], F32)
    XnT = sb("XnT", [128, 10 * D], BF16)
    Xn = [sb("Xn0", [128, D], BF16), sb("Xn1", [128, D], BF16)]
    wS = [sb("wSA", [128, 8192], BF16), sb("wSB", [128, 8192], BF16)]
    W2b = sb("W2b", [128, 128], BF16)
    b1T = sb("b1T", [128, 16], F32)
    B2b = sb("B2b", [128, 8], F32)
    Selb = sb("Selb", [8, 512], BF16)
    idb = sb("idb", [128, 128], BF16)
    Mqb = sb("Mqb", [68, 128], BF16)
    Mqs = [Mqb[0:4, :], Mqb[32:36, :], Mqb[64:68, :]]
    Mkb = sb("Mkb", [68, 384], BF16)
    Mk = [Mkb[0:4, :], Mkb[32:36, :], Mkb[64:68, :]]
    stats = sb("stats", [128, 6 * 10], F32)
    lgacc = sb("lgacc", [128, 8 * 10], F32)
    alphB = sb("alphB", [128, 8 * 10], BF16)
    asf = [sb("asf0", [128, 8], F32), sb("asf1", [128, 8], F32)]
    asc = sb("asc", [128, 4 * 10], F32)

    pg = [psb("pg0", [128, 512], F32), psb("pg1", [128, 512], F32)]
    pg2s = psb("pg2s", [128, 512], F32)

    sems = {}

    def S(name):
        if name not in sems:
            sems[name] = Sem(ctx.enter_context(nc.semaphore(name)))
        return sems[name]

    p = Prog(nc)
    if cfg.get("selfsem", True):
        p.selfsems = {"vector": S("vself"), "scalar": S("sself")}

    XS = lambda l: Xb[:, l * D:(l + 1) * D]
    XnTS = lambda l, c: XnT[:, l * D + c * 128: l * D + (c + 1) * 128]

    def xnt_rhs(l0, ntile, c):
        pstr = XnT.ap().ap[0][0]
        return bass.AP(XnT, l0 * D + c * 128, [[pstr, 128], [D, ntile], [1, 128]])

    def slot_dst(slot, nchunk, width, base=0):
        pstr = slot.ap().ap[0][0]
        return bass.AP(slot, base, [[pstr, 128], [width, nchunk], [1, width]])

    def wpiece_cols(dram, col0, width=512):
        ncols = dram.shape[1]
        return bass.AP(dram, col0, [[ncols, 128], [128 * ncols, 16], [1, width]])

    def wpiece_cols_h(dram, col0, h, width=512):
        ncols = dram.shape[1]
        return bass.AP(dram, col0 + h * 8 * 128 * ncols,
                       [[ncols, 128], [128 * ncols, 8], [1, width]])

    def wpiece_rows(dram, row0):
        return bass.AP(dram, row0 * D, [[D, 128], [128 * D, 4], [1, D]])

    def wpiece_rows_h(dram, row0):
        return bass.AP(dram, row0 * D, [[D, 128], [128 * D, 2], [1, D]])

    s_dx, s_dy, s_dc = S("dx"), S("dy"), S("dc")
    s_dwS = {"A": S("dwA"), "B": S("dwB"), "C": S("dwC")}
    dxv = {}
    wdn = {}

    def dma_w(tag, slot_key, dst_ap, src_ap):
        sem = s_dwS[slot_key]
        p.op("gpsimd", lambda e, dd=dst_ap, ss=src_ap: e.dma_start(dd, ss),
             inc=sem, inc_n=16)
        wdn[tag] = (sem, sem.n)

    def wait_w(eng, tag):
        sem, v = wdn[tag]
        p.wait(eng, sem, v)


    cv = {}

    def ph_consts():
        pairs = [("b1t", b1T[:, :], db1[:, :]), ("b2b", B2b[:, :], db2[:, :]),
                 ("selb", Selb[:, :], dsel[:, :]), ("idb", idb[:, :], didb[:, :]),
                 ("mq0", Mqs[0], dmq[:, :]), ("mq1", Mqs[1], dmq[:, :]),
                 ("mq2", Mqs[2], dmq[:, :]),
                 ("mk0", Mk[0], dmk[0:4, :]), ("mk1", Mk[1], dmk[4:8, :]),
                 ("mk2", Mk[2], dmk[8:12, :])]
        for name, dst, src in pairs:
            p.op("sync", lambda e, dst=dst, src=src: e.dma_start(
                dst, src), inc=s_dc, inc_n=16)
            cv[name] = s_dc.n
        dma_w(("w2p",), "C", bass.AP(W2b, 0, [[W2b.ap().ap[0][0], 128], [1, 128]]),
              dw2[:, :])

    def ph_loadx(c):
        if c == 1:
            p.wait("sync", s_dy, 16 * 8)
        for l in range(10):
            g = 8 * c + l
            sem = S(f"dx{l}")
            p.op("sync", lambda e, l=l, g=g: e.dma_start(
                XS(l), dx[g * 128:(g + 1) * 128, :]), inc=sem, inc_n=16)
            dxv[(c, l)] = (sem, sem.n)

    def ln_stats(l, sk, gates, mavd):
        s_st, s_ma, s_gc = S("st"), S("ma"), S("gc")
        sl = lambda k: stats[:, l * 6 + k: l * 6 + k + 1]
        for (gs, gv) in gates:
            p.wait("scalar", gs, gv)
            p.wait("vector", gs, gv)
        p.op("vector", lambda e, l=l, a=sl(0): e.reduce_sum(
            a, XS(l), AX.X), inc=s_st)
        p.op("scalar", lambda e, l=l, sk=sk, a=sl(1): e.activation(
            sk, XS(l), AF.Square, accum_out=a), inc=s_st)
        p.wait("gpsimd", s_st)

        def gch(fn):
            p.op("gpsimd", fn, inc=s_gc)
            p.wait("gpsimd", s_gc)
        gch(lambda e, a=sl(2), b=sl(0): e.tensor_scalar_mul(a, b, INV_D))
        gch(lambda e, a=sl(3), b=sl(1): e.tensor_scalar_mul(a, b, INV_D))
        gch(lambda e, a=sl(0), b=sl(2): e.tensor_mul(a, b, b))
        gch(lambda e, a=sl(3), b=sl(0): e.tensor_sub(a, a, b))
        p.op("gpsimd", lambda e, a=sl(3): e.tensor_scalar_add(a, a, EPS), inc=s_ma)
        mavd[l] = s_ma.n

    def make_ln(c, pT, sink, wait_dx=False, premav=None):
        s_st, s_ma, s_rs, s_ap = S("st"), S("ma"), S("rs"), S("ap")
        s_tr, s_te = S("tr"), S("te")
        sl = lambda k, l: stats[:, l * 6 + k: l * 6 + k + 1]
        state = {"trb": [None, None], "te": [None] * len(pT), "pend": []}
        mav = dict(premav) if premav is not None else {}
        rsv = {}
        te = {}

        def stage1(l):
            gates = [dxv[(c, l)]] if wait_dx else []
            ln_stats(l, sink, gates, mav)

        def stage2(l):
            xn = Xn[l % 2]
            p.wait("scalar", s_ma, mav[l])
            p.op("scalar", lambda e, l=l: e.activation(
                sl(1, l), sl(3, l), AF.Sqrt), inc=s_rs)
            rsv[l] = s_rs.n
            p.wait("vector", s_rs, rsv[l])
            p.op("vector", lambda e, l=l: e.reciprocal(sl(4, l), sl(1, l)))
            if state["trb"][l % 2] is not None:
                p.wait("vector", s_tr, state["trb"][l % 2])
            p.op("vector", lambda e, l=l, xn=xn: e.tensor_scalar(
                xn[:, :], XS(l), sl(2, l), sl(4, l), ALU.subtract, ALU.mult),
                 inc=s_ap)
            p.wait("tensor", s_ap)
            for half in range(2):
                bi = half % len(pT)
                if state["te"][bi] is not None:
                    p.wait("tensor", s_te, state["te"][bi])
                for j in range(8):
                    cc = half * 8 + j
                    p.op("tensor", lambda e, xn=xn, cc=cc, bi=bi, j=j: e.transpose(
                        pT[bi][:, j * 128:(j + 1) * 128],
                        xn[:, cc * 128:(cc + 1) * 128], idb[:, :]),
                        inc=(s_tr if j == 7 else None))
                p.wait("vector", s_tr)
                p.op("vector", lambda e, l=l, half=half, bi=bi: e.tensor_copy(
                    XnT[:, l * D + half * 1024: l * D + (half + 1) * 1024],
                    pT[bi][:, :]), inc=s_te)
                state["te"][bi] = s_te.n
            state["trb"][l % 2] = s_tr.n
            te[l] = s_te.n

        def emit(l):
            if premav is not None:
                stage2(l)
                return
            if state["pend"]:
                stage2(state["pend"].pop(0))
            stage1(l)
            state["pend"].append(l)

        def drain():
            while state["pend"]:
                stage2(state["pend"].pop(0))

        emit.te = te
        emit.sem = s_te
        emit.drain = drain
        return emit

    def ph_ffn_w1(c, pid, wb, tiles, h_s, plg, ln=None):
        s_hmm, s_hev, s_lgm, s_lge = S("hmm"), S("hev"), S("lgm"), S("lge")
        ob = 4096 - wb
        n = len(tiles)
        l0 = tiles[0]
        tsubs = [(l0 + 4 * i, min(4, n - 4 * i)) for i in range((n + 3) // 4)]
        p.wait("vector", s_dc)
        for l in tiles:
            p.op("vector", lambda e, l=l: e.tensor_copy(
                lgacc[:, l * 8:(l + 1) * 8], B2b[:, :]))
        wait_w("tensor", ("w2p",))
        hev_bank = [None, None]
        hev_end = {}

        def hsl(g, s4, a, b):
            off = (g % 2) * 5120 + s4 * 1280
            return h_s[:, off + a: off + b]

        def emit_logits(g, i, l):
            p.wait("tensor", s_lge)
            p.wait("tensor", s_hev, hev_end[g])
            for s4 in range(2):
                p.op("tensor", lambda e, l=l, i=i, s4=s4, g=g: e.matmul(
                    plg[:, l * 8:(l + 1) * 8],
                    hsl(g, s4, i * 128, (i + 1) * 128),
                    W2b[:, (g * 2 + s4) * 8:(g * 2 + s4 + 1) * 8],
                    start=(s4 == 0), stop=(s4 == 1)),
                    inc=(s_lgm if s4 == 1 else None))
            p.wait("vector", s_lgm)
            p.op("vector", lambda e, l=l: e.tensor_add(
                lgacc[:, l * 8:(l + 1) * 8], lgacc[:, l * 8:(l + 1) * 8],
                plg[:, l * 8:(l + 1) * 8]), inc=s_lge)

        if ln is not None:
            for l in tiles[:4]:
                ln(l)
        lnq = list(tiles[4:]) if ln is not None else []
        for g in range(8):
            slot = wS[g % 2]
            if g < 2:
                tag = ("w1", c, pid, g)
                if tag not in wdn:
                    dma_w(tag, "AB"[g % 2], slot_dst(slot, 16, 256, wb),
                          wpiece_cols(dw1, g * 256, 256))
                if g == 1:
                    for h in range(2):
                        dma_w(("vfh", c, pid, h), "AB"[h],
                              slot_dst(wS[h], 8, 512, ob),
                              bass.AP(dvf, h * 8 * 128 * 512,
                                      [[512, 128], [128 * 512, 8], [1, 512]]))
            wait_w("tensor", ("w1", c, pid, g))
            if g == 0:
                pieces = [(s4, t0, nt) for (t0, nt) in tsubs for s4 in range(2)]
            else:
                pieces = [(s4, t0, nt) for s4 in range(2) for (t0, nt) in tsubs]
            for pi, (s4, t0, nt) in enumerate(pieces):
                bank = pg[pi % 2]
                if ln is not None:
                    while any(l2 not in ln.te for l2 in range(t0, t0 + nt)):
                        if lnq:
                            ln(lnq.pop(0))
                        else:
                            ln.drain()
                    p.wait("tensor", ln.sem, max(ln.te[l2]
                                                 for l2 in range(t0, t0 + nt)))
                if hev_bank[pi % 2] is not None:
                    p.wait("tensor", s_hev, hev_bank[pi % 2])
                for cc in range(16):
                    p.op("tensor", lambda e, bank=bank, slot=slot, cc=cc, s4=s4,
                         t0=t0, nt=nt: e.matmul(
                             bank[:, 0:nt * 128],
                             slot[:, wb + cc * 256 + s4 * 128:
                                  wb + cc * 256 + (s4 + 1) * 128],
                             xnt_rhs(t0, nt, cc),
                             start=(cc == 0), stop=(cc == 15)),
                         inc=(s_hmm if cc == 15 else None))
                if g > 0 and pi < n:
                    emit_logits(g - 1, pi, tiles[pi])
                if g == 0 and lnq:
                    ln(lnq.pop(0))
                p.wait("scalar", s_hmm)
                toff = (t0 - l0) * 128
                p.op("scalar", lambda e, bank=bank, s4=s4, toff=toff, nt=nt, g=g:
                     e.activation(hsl(g, s4, toff, toff + nt * 128),
                                  bank[:, 0:nt * 128], AF.Relu,
                                  bias=b1T[:, g * 2 + s4: g * 2 + s4 + 1]),
                     inc=s_hev)
                hev_bank[pi % 2] = s_hev.n
            if g > 0:
                for i2 in range(len(pieces), n):
                    emit_logits(g - 1, i2, tiles[i2])
            hev_end[g] = s_hev.n
            if g + 2 < 8:
                p.wait("gpsimd", s_hmm)
                dma_w(("w1", c, pid, g + 2), "AB"[g % 2],
                      slot_dst(wS[(g + 2) % 2], 16, 256, wb),
                      wpiece_cols(dw1, (g + 2) * 256, 256))
            if g >= 6:
                p.wait("gpsimd", s_hmm)
                h = g - 6
                dma_w(("ufh", c, pid, h), "AB"[h], slot_dst(wS[h], 2, D, wb),
                      bass.AP(duf, h * 256 * D, [[D, 128], [128 * D, 2], [1, D]]))
        return [(lambda i=i, l=l: emit_logits(7, i, l))
                for i, l in enumerate(tiles)]

    uxv = {}

    def ph_ffn_vu(c, pid, wb, tiles, z4, Ab4, aT, pAT, tail=None, prefetch=None,
                  stats_cb=None):
        ob = 4096 - wb
        s_alm, s_ale, s_alx = S("alm"), S("ale"), S("alx")
        s_vm, s_ve, s_um, s_ux = S("vm"), S("ve"), S("um"), S("ux")
        s_atm, s_ate, s_abm, s_abe = S("atm"), S("ate"), S("abm"), S("abe")
        s_hev = S("hev")
        n = len(tiles)
        l0 = tiles[0]
        tails = list(tail or [])
        alxv = {}
        alx_after = {}

        def router(i, l):
            par = i % 2
            sl = lambda k, l: asc[:, l * 4 + k: l * 4 + k + 1]
            p.op("vector", lambda e, l=l: e.reduce_max(
                sl(0, l), lgacc[:, l * 8:(l + 1) * 8], AX.X, negate=True), inc=s_alm)
            if alx_after.get(par) is not None:
                p.wait("scalar", s_alx, alx_after[par])
            p.wait("scalar", s_alm)
            p.op("scalar", lambda e, l=l, par=par: e.activation(
                asf[par][:, :], lgacc[:, l * 8:(l + 1) * 8], AF.Exp,
                bias=sl(0, l), accum_out=sl(1, l)), inc=s_ale)
            p.wait("vector", s_ale)
            p.op("vector", lambda e, l=l: e.reciprocal(sl(2, l), sl(1, l)))
            p.op("vector", lambda e, l=l, par=par: e.tensor_scalar_mul(
                alphB[:, l * 8:(l + 1) * 8], asf[par][:, :], sl(2, l)), inc=s_alx)
            alx_after[par] = s_alx.n
            alxv[l] = s_alx.n

        def alpha_sel(i0, nt):
            p.wait("tensor", s_alx, max(alxv[tiles[i0 + tt]] for tt in range(nt)))
            p.wait("tensor", s_ate)
            for tt in range(nt):
                l = tiles[i0 + tt]
                p.op("tensor", lambda e, l=l, tt=tt: e.transpose(
                    pAT[:, tt * 128:(tt + 1) * 128],
                    alphB[:, l * 8:(l + 1) * 8], idb[:, :]),
                    inc=(s_atm if tt == nt - 1 else None))
            p.wait("vector", s_atm)
            p.wait("vector", s_abm)
            p.op("vector", lambda e, nt=nt: e.tensor_copy(
                aT[:, 0:nt * 128], pAT[:, 0:nt * 128]), inc=s_ate)
            p.wait("tensor", s_ate)
            for kc2 in range(4):
                p.wait("tensor", s_abe)
                p.op("tensor", lambda e, kc2=kc2, nt=nt: e.matmul(
                    pg2s[:, 0:nt * 128],
                    Selb[:, kc2 * 128:(kc2 + 1) * 128], aT[:, 0:nt * 128],
                    start=True, stop=True), inc=s_abm)
                p.wait("vector", s_abm)
                p.op("vector", lambda e, kc2=kc2, nt=nt: e.tensor_copy(
                    Ab4[:, kc2 * 512: kc2 * 512 + nt * 128], pg2s[:, 0:nt * 128]),
                    inc=s_abe)
            return s_abe.n

        tsubs = [(4 * i, min(4, n - 4 * i)) for i in range((n + 3) // 4)]
        p.wait("tensor", s_dc)
        wait_w("tensor", ("vfh", c, pid, 0))
        wait_w("tensor", ("vfh", c, pid, 1))
        hev0 = s_hev.n
        ve_bank = [None, None]
        ux_bank = [None, None]
        pf = list(prefetch or [])
        for ti, (i0, nt) in enumerate(tsubs):
            first = (ti == 0)
            vmv = {}
            st = {"abe": None}

            def vx_mm(kc, splice=False, i0=i0, nt=nt, first=first, ti=ti,
                      vmv=vmv):
                bank = pg[kc % 2]
                p.wait("tensor", s_ux)
                if ve_bank[kc % 2] is not None:
                    p.wait("tensor", s_ve, ve_bank[kc % 2])
                if first and kc < 2:
                    p.wait("tensor", s_hev, hev0)
                for cc in range(16):
                    half, ci = cc // 8, cc % 8
                    p.op("tensor", lambda e, bank=bank, kc=kc, half=half, ci=ci,
                         cc=cc: e.matmul(
                             bank[:, 0:nt * 128],
                             wS[half][:, ob + ci * 512 + kc * 128:
                                      ob + ci * 512 + (kc + 1) * 128],
                             xnt_rhs(l0 + i0, nt, cc),
                             start=(cc == 0), stop=(cc == 15)),
                         inc=(s_vm if cc == 15 else None))
                    if splice and cc % 3 == 2 and tails:
                        tails.pop(0)()
                vmv[kc] = s_vm.n
                if ti == len(tsubs) - 1 and kc == 3 and pf:
                    p.wait("gpsimd", s_vm)
                    while pf:
                        dma_w(*pf.pop(0))

            def z_evac(kc, nt=nt, vmv=vmv, st=st):
                bank = pg[kc % 2]
                p.wait("vector", s_vm, vmv[kc])
                p.wait("vector", s_abe, st["abe"])
                p.wait("vector", s_um)
                p.op("vector", lambda e, bank=bank, kc=kc: e.scalar_tensor_tensor(
                    z4[:, kc * 512: kc * 512 + nt * 128], bank[:, 0:nt * 128], 0.5,
                    Ab4[:, kc * 512: kc * 512 + nt * 128], ALU.mult, ALU.mult),
                    inc=s_ve)
                ve_bank[kc % 2] = s_ve.n

            vx_mm(0, splice=first)
            if first:
                vx_mm(1, splice=True)
                while tails:
                    tails.pop(0)()
                for i, l in enumerate(tiles):
                    router(i, l)
                st["abe"] = alpha_sel(i0, nt)
            else:
                st["abe"] = alpha_sel(i0, nt)
                vx_mm(1)
            z_evac(0)
            z_evac(1)
            vx_mm(2)
            z_evac(2)
            vx_mm(3)
            z_evac(3)
            wait_w("tensor", ("ufh", c, pid, 0))
            wait_w("tensor", ("ufh", c, pid, 1))
            p.wait("tensor", s_ve)
            for tt in range(nt):
                l = tiles[i0 + tt]
                for cg in range(4):
                    bank = pg[cg % 2]
                    if ux_bank[cg % 2] is not None:
                        p.wait("tensor", s_ux, ux_bank[cg % 2])
                    for kc in range(4):
                        p.op("tensor", lambda e, bank=bank, tt=tt, cg=cg, kc=kc:
                             e.matmul(bank[:, :],
                                      z4[:, kc * 512 + tt * 128: kc * 512 + (tt + 1) * 128],
                                      wS[kc // 2][:, wb + (kc % 2) * D + cg * 512:
                                                  wb + (kc % 2) * D + (cg + 1) * 512],
                                      start=(kc == 0), stop=(kc == 3)),
                             inc=(s_um if kc == 3 else None))
                    p.wait("vector", s_um)
                    p.op("vector", lambda e, bank=bank, l=l, cg=cg: e.tensor_add(
                        Xb[:, l * D + cg * 512: l * D + (cg + 1) * 512],
                        Xb[:, l * D + cg * 512: l * D + (cg + 1) * 512],
                        bank[:, :]), inc=s_ux)
                    ux_bank[cg % 2] = s_ux.n
                uxv[l] = s_ux.n
                if stats_cb is not None:
                    stats_cb(l)

    def ph_attn(c, hg, A, ln=None, next_a=None, stats_cb=None):
        (Kb, Vt, Qb, Praw, Pn, PTs, aoAll, att4, pS, pPT) = A
        s_pm, s_peA, s_peV = S("pm"), S("peA"), S("peV")
        s_sm, s_nm, s_px, s_pn, s_ptm, s_ptd, s_aom, s_aoev, s_wom, s_wx = (
            S("sm"), S("nm"), S("px"), S("pn"), S("ptm"), S("ptd"), S("aom"),
            S("aoev"), S("wom"), S("wx"))
        tsubs = [(0, 4), (4, 4), (8, 2)]
        tiles10 = list(range(10))
        for h in range(2):
            tag = ("wk", c, hg, h)
            if tag not in wdn:
                dma_w(tag, "AB"[h], slot_dst(wS[h], 8, 512, 4096),
                      wpiece_cols_h(dwk, hg * 512, h))
        for h in range(2):
            dma_w(("wq", c, hg, h), "AB"[h], slot_dst(wS[h], 8, 512, 0),
                  wpiece_cols_h(dwq, hg * 512, h))
        p.wait("tensor", s_dc)

        pm_bank = [None, None]
        pmix = [0]
        kpe = {}
        vpe = {}
        qpe = {}

        def proj_piece(kind, *args):
            b = pmix[0] % 2
            pmix[0] += 1
            bank = pg[b]
            if pm_bank[b] is not None:
                p.wait("tensor", pm_bank[b][0], pm_bank[b][1])
            if kind == "k":
                h, (i0, nt) = args
                wait_w("tensor", ("wk", c, hg, 0))
                wait_w("tensor", ("wk", c, hg, 1))
                for cc in range(16):
                    p.op("tensor", lambda e, bank=bank, h=h, cc=cc, i0=i0, nt=nt:
                         e.matmul(bank[:, 0:nt * 128],
                                  wS[cc // 8][:, 4096 + (cc % 8) * 512 + h * 128:
                                              4096 + (cc % 8) * 512 + (h + 1) * 128],
                                  xnt_rhs(i0, nt, cc),
                                  start=(cc == 0), stop=(cc == 15)),
                         inc=(s_pm if cc == 15 else None))
                p.wait("scalar", s_pm)
                p.op("scalar", lambda e, bank=bank, h=h, i0=i0, nt=nt: e.copy(
                    Kb[:, h * 1280 + i0 * 128: h * 1280 + (i0 + nt) * 128],
                    bank[:, 0:nt * 128]), inc=s_peA)
                kpe[(h, i0)] = s_peA.n
                pm_bank[b] = (s_peA, s_peA.n)
            elif kind == "v":
                l, = args
                wait_w("tensor", ("wv", c, hg, 0))
                wait_w("tensor", ("wv", c, hg, 1))
                for cc in range(16):
                    p.op("tensor", lambda e, bank=bank, l=l, cc=cc: e.matmul(
                        bank[:, :], XnTS(l, cc),
                        wS[cc // 8][:, 4096 + (cc % 8) * 512:
                                    4096 + (cc % 8 + 1) * 512],
                        start=(cc == 0), stop=(cc == 15)),
                        inc=(s_pm if cc == 15 else None))
                p.wait("scalar", s_pm)
                p.op("scalar", lambda e, bank=bank, l=l: e.copy(
                    Vt[:, l * 512:(l + 1) * 512], bank[:, :]), inc=s_peA)
                vpe[l] = s_peA.n
                pm_bank[b] = (s_peA, s_peA.n)
            else:
                h, (i0, nt) = args
                wait_w("tensor", ("wq", c, hg, 0))
                wait_w("tensor", ("wq", c, hg, 1))
                for cc in range(16):
                    p.op("tensor", lambda e, bank=bank, h=h, cc=cc, i0=i0, nt=nt:
                         e.matmul(bank[:, 0:nt * 128],
                                  wS[cc // 8][:, (cc % 8) * 512 + h * 128:
                                              (cc % 8) * 512 + (h + 1) * 128],
                                  xnt_rhs(i0, nt, cc),
                                  start=(cc == 0), stop=(cc == 15)),
                         inc=(s_pm if cc == 15 else None))
                p.wait("vector", s_pm)
                p.op("vector", lambda e, bank=bank, h=h, i0=i0, nt=nt:
                     e.tensor_scalar_mul(
                         Qb[:, h * 1024 + (i0 - 1) * 128: h * 1024 + (i0 - 1 + nt) * 128],
                         bank[:, 0:nt * 128], QSCALE), inc=s_peV)
                qpe[(h, i0)] = s_peV.n
                pm_bank[b] = (s_peV, s_peV.n)

        at = lambda k, par: att4[:, par * 8 + k: par * 8 + k + 1]
        at2 = lambda k, par: att4[:, par * 8 + k: par * 8 + k + 2]
        qtiles = list(cfg.get("qtiles", range(1, 9)))
        its = [(q, hp) for q in qtiles for hp in range(2)]
        n_it = len(its)
        smv = {}; nmv = {}; pxv = {}; pnv = {}; ptmv = {}; ptdv = {}
        aomv = {}; aoevv = {}
        wo_read = {}

        def st_a(i):
            q, hp = its[i]
            par = i % 2
            bank = pS[par]
            gq = 8 * c + q
            mi = 0 if gq == 1 else (2 if gq == 16 else 1)
            mkt, mqt = Mk[mi], Mqs[mi]
            qts = 1 if q <= 4 else 5
            p.wait("tensor", s_peV, qpe[(2 * hp + 1, qts)])
            p.wait("tensor", s_peA, kpe[(2 * hp + 1, 8)])
            if i - 2 >= 0:
                p.wait("tensor", s_px, pxv[i - 2])
            for hh in range(2):
                h = 2 * hp + hh
                p.op("tensor", lambda e, bank=bank, mkt=mkt, mqt=mqt, hh=hh:
                     e.matmul(bank[:, hh * 512: hh * 512 + 384], mqt, mkt,
                              start=True, stop=False))
                p.op("tensor", lambda e, bank=bank, h=h, q=q, hh=hh: e.matmul(
                    bank[:, hh * 512: hh * 512 + 384],
                    Qb[:, h * 1024 + (q - 1) * 128: h * 1024 + q * 128],
                    Kb[:, h * 1280 + (q - 1) * 128: h * 1280 + (q - 1) * 128 + 384],
                    start=False, stop=True), inc=(s_sm if hh == 1 else None))
            smv[i] = s_sm.n

        def st_b(i):
            par = i % 2
            bank = pS[par]
            p.wait("vector", s_sm, smv[i])
            if i - 2 >= 0:
                p.wait("vector", s_px, pxv[i - 2])
            p.op("vector", lambda e, bank=bank, par=par: e.reduce_max(
                at(0, par),
                bass.AP(bank.tensor if hasattr(bank, "tensor") else bank, 0,
                        [[bank.ap().ap[0][0], 128], [512, 2], [1, 384]]),
                AX.XY, negate=True), inc=s_nm)
            nmv[i] = s_nm.n
            p.wait("scalar", s_nm, nmv[i])
            if i - 2 >= 0:
                p.wait("scalar", s_pn, pnv[i - 2])
            for hh in range(2):
                p.op("scalar", lambda e, bank=bank, par=par, hh=hh: e.activation(
                    Praw[par][:, hh * 384:(hh + 1) * 384],
                    bank[:, hh * 512: hh * 512 + 384], AF.Exp, bias=at(0, par),
                    accum_out=at(1 + hh, par)), inc=(s_px if hh == 1 else None))
            pxv[i] = s_px.n
            p.wait("vector", s_px, pxv[i])
            p.op("vector", lambda e, par=par: e.reciprocal(at2(3, par), at2(1, par)))
            if i - 2 >= 0:
                p.wait("vector", s_ptm, ptmv[i - 2])
            for hh in range(2):
                p.op("vector", lambda e, par=par, hh=hh: e.tensor_scalar_mul(
                    Pn[par][:, hh * 384:(hh + 1) * 384],
                    Praw[par][:, hh * 384:(hh + 1) * 384], at(3 + hh, par)),
                    inc=(s_pn if hh == 1 else None))
            pnv[i] = s_pn.n

        def st_c1(i):
            q, hp = its[i]
            par = i % 2
            p.wait("tensor", s_pn, pnv[i])
            if i - 1 >= 0:
                p.wait("tensor", s_ptd, ptdv[i - 1])
            elif ln is not None:
                p.wait("tensor", ln.sem)
            for kt in range(6):
                p.op("tensor", lambda e, par=par, kt=kt: e.transpose(
                    pPT[:, kt * 128:(kt + 1) * 128],
                    Pn[par][:, kt * 128:(kt + 1) * 128], idb[:, :]),
                    inc=(s_ptm if kt == 5 else None))
            ptmv[i] = s_ptm.n
            eng = "scalar" if i % 2 else "vector"
            p.wait(eng, s_ptm, ptmv[i])
            if i - 2 >= 0:
                p.wait(eng, s_aom, aomv[i - 2])
            if eng == "scalar":
                p.op(eng, lambda e, par=par: e.copy(PTs[par][:, :], pPT[:, 0:768]),
                     inc=s_ptd)
            else:
                p.op(eng, lambda e, par=par: e.tensor_copy(PTs[par][:, :], pPT[:, 0:768]),
                     inc=s_ptd)
            ptdv[i] = s_ptd.n

        def st_c2(i):
            q, hp = its[i]
            par = i % 2
            qi = qtiles.index(q)
            p.wait("tensor", s_ptd, ptdv[i])
            p.wait("tensor", s_peA, vpe[min(q + 1, 9)])
            if hp == 0 and qi - 1 >= 0:
                p.wait("tensor", s_aoev, aoevv[qtiles[qi - 1]])
            for j in range(6):
                hh, kt = j // 3, j % 3
                h = 2 * hp + hh
                p.op("tensor", lambda e, par=par, h=h, hh=hh, q=q, kt=kt: e.matmul(
                    pg2s[:, h * 128:(h + 1) * 128],
                    Vt[:, (q - 1 + kt) * 512 + h * 128:
                       (q - 1 + kt) * 512 + (h + 1) * 128],
                    pPT[:, :] if False else PTs[par][:, (hh * 3 + kt) * 128:(hh * 3 + kt + 1) * 128],
                    start=(kt == 0), stop=(kt == 2)),
                    inc=(s_aom if j == 5 else None))
            aomv[i] = s_aom.n

        def st_d(i):
            q, hp = its[i]
            qi = qtiles.index(q)
            slot = qi % 4
            p.wait("scalar", s_aom, aomv[i])
            if wo_read.get(slot) is not None:
                p.wait("scalar", s_wom, wo_read[slot])
            p.op("scalar", lambda e, slot=slot: e.copy(
                aoAll[:, slot * 512:(slot + 1) * 512], pg2s[:, :]), inc=s_aoev)
            aoevv[q] = s_aoev.n

        wx_bank = [None, None]

        def wo_section(qs):
            p.wait("tensor", s_peA)
            p.wait("tensor", s_peV)
            for q in qs:
                qi = qtiles.index(q)
                slot = qi % 4
                p.wait("tensor", s_aoev, aoevv[q])
                for cg in range(4):
                    bsel = cg % 2
                    wbank = pg[bsel]
                    if wx_bank[bsel] is not None:
                        p.wait("tensor", s_wx, wx_bank[bsel])
                    for h2 in range(4):
                        p.op("tensor", lambda e, wbank=wbank, slot=slot, h2=h2, cg=cg:
                             e.matmul(wbank[:, :],
                                      aoAll[:, slot * 512 + h2 * 128: slot * 512 + (h2 + 1) * 128],
                                      wS[h2 // 2][:, (h2 % 2) * D + cg * 512:
                                                  (h2 % 2) * D + (cg + 1) * 512],
                                      start=(h2 == 0), stop=(h2 == 3)),
                             inc=(s_wom if h2 == 3 else None))
                    wv = s_wom.n
                    p.wait("vector", s_wom, wv)
                    p.op("vector", lambda e, wbank=wbank, q=q, cg=cg: e.tensor_add(
                        Xb[:, q * D + cg * 512: q * D + (cg + 1) * 512],
                        Xb[:, q * D + cg * 512: q * D + (cg + 1) * 512],
                        wbank[:, :]), inc=s_wx)
                    wx_bank[bsel] = s_wx.n
                wo_read[slot] = s_wom.n
                if stats_cb is not None:
                    stats_cb(q, (s_wx, s_wx.n))

        def ln_wait(t0, nt):
            if ln is not None:
                if any(l2 not in ln.te for l2 in range(t0, t0 + nt)):
                    ln.drain()
                p.wait("tensor", ln.sem, max(ln.te[l2] for l2 in range(t0, t0 + nt)))

        if ln is not None:
            for l in tiles10[:4]:
                if l not in ln.te:
                    ln(l)
        lnq = [l for l in tiles10[4:] if l not in ln.te] if ln is not None else []
        korder = [("k", h, ts) for ts in tsubs for h in range(4)]
        for (kind, h, ts) in korder:
            ln_wait(*ts)
            proj_piece("k", h, ts)
            if lnq:
                ln(lnq.pop(0))
        pmK = s_pm.n
        p.wait("gpsimd", s_pm, pmK)
        for h in range(2):
            dma_w(("wv", c, hg, h), "AB"[h], slot_dst(wS[h], 8, 512, 4096),
                  wpiece_cols_h(dwv, hg * 512, h))
        if ln is not None:
            ln.drain()
            p.wait("tensor", ln.sem)
        for h in range(4):
            proj_piece("q", h, (1, 4))
            proj_piece("q", h, (5, 4))
        vq = list(range(10))
        for l in range(3):
            proj_piece("v", vq.pop(0))
        p.wait("gpsimd", s_pm)
        for h in range(2):
            dma_w(("wo", c, hg, h), "AB"[h], slot_dst(wS[h], 2, D, 0),
                  wpiece_rows_h(dwo, hg * 512 + h * 256))
        for i in range(n_it + 3):
            if i < n_it:
                st_a(i)
            if vq:
                proj_piece("v", vq.pop(0))
                if not vq and next_a is not None:
                    p.wait("gpsimd", s_pm)
                    for spec in next_a:
                        dma_w(*spec)
            if 0 <= i - 1 < n_it:
                st_b(i - 1)
            if 0 <= i - 2 < n_it:
                st_c1(i - 2)
            if 0 <= i - 3 < n_it:
                st_c2(i - 3)
                if its[i - 3][1] == 1:
                    st_d(i - 3)
                    qdone = qtiles.index(its[i - 3][0])
                    if qdone % 2 == 1:
                        wait_w("tensor", ("wo", c, hg, 0))
                        wait_w("tensor", ("wo", c, hg, 1))
                        wo_section(qtiles[qdone - 1:qdone + 1])
        if len(qtiles) % 2 == 1:
            wait_w("tensor", ("wo", c, hg, 0))
            wait_w("tensor", ("wo", c, hg, 1))
            wo_section(qtiles[-1:])

    def ph_store(c, load_next=False, stats_cb=None):
        s_ux = S("ux")
        dy0 = s_dy.n
        for q in range(1, 9):
            g = 8 * c + q
            p.wait("sync", s_ux, uxv[q])
            p.op("sync", lambda e, q=q, g=g: e.dma_start(
                dy[(g - 1) * 128:g * 128, :], XS(q)), inc=s_dy, inc_n=16)
        if load_next:
            pend = []
            for l in [0, 9] + list(range(1, 9)):
                g = 8 * (c + 1) + l
                if 1 <= l <= 8:
                    p.wait("gpsimd", s_dy, dy0 + 16 * l)
                sem = S(f"dx{l}")
                p.op("gpsimd", lambda e, l=l, g=g: e.dma_start(
                    XS(l), dx[g * 128:(g + 1) * 128, :]), inc=sem, inc_n=16)
                dxv[(c + 1, l)] = (sem, sem.n)
                if stats_cb is not None:
                    pend.append(l)
                    if len(pend) >= 3:
                        stats_cb(pend.pop(0))
            while pend:
                stats_cb(pend.pop(0))

    def ph_tap(tiles):
        p.wait("sync", S("ux"))
        p.wait("sync", S("wx"))
        for l in tiles:
            p.op("sync", lambda e, l=l: e.dma_start(
                dtap[l * 128:(l + 1) * 128, :], XS(l)), inc=s_dy, inc_n=16)
        p.wait("sync", s_dy)

    nchunks = 1 if stop_after else 2
    uid = [0]

    def un(base):
        uid[0] += 1
        return f"{base}_{uid[0]}"

    ph_consts()
    ln1mav = {}
    for c in range(nchunks):
        ln2mav = {}
        ln3mav = {}
        if (c, 0) not in dxv:
            ph_loadx(c)
        with ExitStack() as fs:
            h_s = fs.enter_context(nc.sbuf_tensor(un("h_s"), [128, 2 * 4 * 1280], BF16))
            z4 = fs.enter_context(nc.sbuf_tensor(un("z4"), [128, 4 * 512], BF16))
            Ab4 = fs.enter_context(nc.sbuf_tensor(un("Ab4"), [128, 4 * 512], BF16))
            aT = fs.enter_context(nc.sbuf_tensor(un("aT"), [8, 512], BF16))
            plg = fs.enter_context(nc.psum_tensor(un("plg"), [128, 80], F32))
            pT0 = fs.enter_context(nc.psum_tensor(un("pT0"), [128, 1024], BF16))
            pT1 = fs.enter_context(nc.psum_tensor(un("pT1"), [128, 1024], BF16))
            ln1 = make_ln(c, [pT0, pT1], h_s[:, 2560:4608], wait_dx=True,
                          premav=(ln1mav if c > 0 else None))
            tail = ph_ffn_w1(c, 1, 0, list(range(10)), h_s, plg, ln=ln1)
            with nc.psum_tensor(un("pAT"), [8, 512], BF16) as pAT:
                pfx = None
                sc2 = None
                if not stop_after:
                    pfx = [(("wk", c, 0, h), "AB"[h],
                            slot_dst(wS[h], 8, 512, 4096),
                            wpiece_cols_h(dwk, 0, h)) for h in range(2)]
                    sc2 = (lambda l, hs=h_s: ln_stats(
                        l, hs[:, 2560:4608], [(S("ux"), uxv[l])], ln2mav))
                ph_ffn_vu(c, 1, 0, list(range(10)), z4, Ab4, aT, pAT, tail=tail,
                          prefetch=pfx, stats_cb=sc2)
                p.flush()
        if stop_after == "s1":
            ph_tap(list(range(10)))
            p.flush()
            break
        with ExitStack() as asx:
            Kb = asx.enter_context(nc.sbuf_tensor(un("Kb"), [128, 4 * 1280], BF16))
            Vt = asx.enter_context(nc.sbuf_tensor(un("Vt"), [128, 10 * 512], BF16))
            Qb = asx.enter_context(nc.sbuf_tensor(un("Qb"), [128, 4 * 1024], BF16))
            Praw = [asx.enter_context(nc.sbuf_tensor(un("Praw"), [128, 768], F32))
                    for _ in range(2)]
            Pn = [asx.enter_context(nc.sbuf_tensor(un("Pn"), [128, 768], BF16))
                  for _ in range(2)]
            PTs = [asx.enter_context(nc.sbuf_tensor(un("PTs"), [128, 768], BF16))
                   for _ in range(2)]
            aoAll = asx.enter_context(nc.sbuf_tensor(un("aoAll"), [128, 4 * 512], BF16))
            att4 = asx.enter_context(nc.sbuf_tensor(un("att4"), [128, 16], F32))
            pS = [asx.enter_context(nc.psum_tensor(un("pS"), [128, 1024], F32))
                  for _ in range(2)]
            pPT = asx.enter_context(nc.psum_tensor(un("pPT"), [128, 1024], BF16))
            A = (Kb, Vt, Qb, Praw, Pn, PTs, aoAll, att4, pS, pPT)
            ln2 = make_ln(c, [pPT], Qb[:, 0:2048],
                          premav=(ln2mav if not stop_after else None))
            for hg in range(cfg.get("hgs", 4)):
                scb = None
                if hg == 3 and not stop_after:
                    na = [(("w1", c, 3, g), "AB"[g], slot_dst(wS[g], 16, 256, 4096),
                           wpiece_cols(dw1, g * 256, 256)) for g in range(2)]
                    scb = (lambda q, gate: ln_stats(q, XS(9), [gate], ln3mav))
                elif hg + 1 < cfg.get("hgs", 4):
                    na = [(("wk", c, hg + 1, h), "AB"[h],
                           slot_dst(wS[h], 8, 512, 4096),
                           wpiece_cols_h(dwk, (hg + 1) * 512, h)) for h in range(2)]
                else:
                    na = None
                ph_attn(c, hg, A, ln=(ln2 if hg == 0 else None), next_a=na,
                        stats_cb=scb)
                p.flush()
        if stop_after == "att":
            ph_tap(list(range(10)))
            p.flush()
            break
        with ExitStack() as fs:
            h_s = fs.enter_context(nc.sbuf_tensor(un("h_s"), [128, 2 * 4 * 1280], BF16))
            z4 = fs.enter_context(nc.sbuf_tensor(un("z4"), [128, 4 * 512], BF16))
            Ab4 = fs.enter_context(nc.sbuf_tensor(un("Ab4"), [128, 4 * 512], BF16))
            aT = fs.enter_context(nc.sbuf_tensor(un("aT"), [8, 512], BF16))
            plg = fs.enter_context(nc.psum_tensor(un("plg"), [128, 80], F32))
            pT0 = fs.enter_context(nc.psum_tensor(un("pT0"), [128, 1024], BF16))
            pT1 = fs.enter_context(nc.psum_tensor(un("pT1"), [128, 1024], BF16))
            ln3 = make_ln(c, [pT0, pT1], h_s[:, 2560:4608], premav=ln3mav)
            tail = ph_ffn_w1(c, 3, 4096, list(range(1, 9)), h_s, plg, ln=ln3)
            with nc.psum_tensor(un("pAT"), [8, 512], BF16) as pAT:
                pfx = None
                sc1 = None
                if c + 1 < nchunks:
                    pfx = [(("w1", c + 1, 1, g), "AB"[g],
                            slot_dst(wS[g], 16, 256, 0),
                            wpiece_cols(dw1, g * 256, 256)) for g in range(2)]
                    sc1 = (lambda l, hs=h_s: ln_stats(
                        l, hs[:, 2560:4608], [dxv[(c + 1, l)]], ln1mav))
                ph_ffn_vu(c, 3, 4096, list(range(1, 9)), z4, Ab4, aT, pAT,
                          tail=tail, prefetch=pfx)
                ph_store(c, load_next=(c + 1 < nchunks), stats_cb=sc1)
                p.flush()
    if not stop_after:
        p.wait("sync", s_dy, 16 * 16)
        p.flush()

    ctx.close()
    return nc



def _build_masks():
    NEGf = np.float32(NEG)
    mq = np.zeros((4, 128), dtype=BF)
    mq[0, :64] = 1
    mq[1, 64:] = 1
    mq[2, :] = 1

    def mk(pad_lo=False, pad_hi=False):
        m = np.zeros((4, 384), dtype=np.float32)
        m[0, 320:] = NEGf
        m[1, :64] = NEGf
        if pad_lo:
            m[2, :128] = NEGf
        if pad_hi:
            m[2, 256:] = NEGf
        return m.astype(BF)

    return mq, mk(), mk(pad_lo=True), mk(pad_hi=True)


def _host_inputs(L0_attn_hidden_states, L0_attn_Wq, L0_attn_Wk, L0_attn_Wv,
                 L0_attn_Wo, L0_ffn_U, L0_ffn_V, L0_ffn_W1, L0_ffn_b1,
                 L0_ffn_W2, L0_ffn_b2):
    x = np.asarray(L0_attn_hidden_states, dtype=np.float32)
    w1 = np.ascontiguousarray(np.asarray(L0_ffn_W1, dtype=np.float32))
    wq = np.ascontiguousarray(np.asarray(L0_attn_Wq, dtype=np.float32))
    wk = np.ascontiguousarray(np.asarray(L0_attn_Wk, dtype=np.float32))
    wv = np.ascontiguousarray(np.asarray(L0_attn_Wv, dtype=np.float32))
    wo = np.ascontiguousarray(np.asarray(L0_attn_Wo, dtype=np.float32))
    U = np.asarray(L0_ffn_U, dtype=np.float32)
    V = np.asarray(L0_ffn_V, dtype=np.float32)
    vflat = np.ascontiguousarray(V.transpose(2, 0, 1).reshape(D, 512))
    uflat = np.ascontiguousarray(U.transpose(0, 2, 1).reshape(512, D))
    W2 = np.asarray(L0_ffn_W2, dtype=np.float32)
    w2p = np.ascontiguousarray(
        W2.reshape(16, 128, 8).transpose(1, 0, 2).reshape(128, 128))
    b1t = np.ascontiguousarray(np.asarray(L0_ffn_b1, np.float32).reshape(16, 128).T)
    b2b = np.ascontiguousarray(
        np.broadcast_to(np.asarray(L0_ffn_b2, np.float32), (128, 8)))
    sel = np.zeros((8, 512), dtype=BF)
    for kr in range(512):
        sel[kr // 64, kr] = 1
    idb = np.eye(128, dtype=BF)
    mq, mk_int, mk_lo, mk_hi = _build_masks()

    shared = dict(w1=w1, wq=wq, wk=wk, wv=wv, wo=wo, vflat=vflat, uflat=uflat,
                  w2p=w2p, b1t=b1t, b2b=b2b, selb=sel, d_idb=idb, d_mq=mq)
    in_maps = []
    for core in range(8):
        b, half = core // 2, core % 2
        xc = np.zeros((NTP * 128, D), dtype=np.float32)
        if half == 0:
            xc[128:] = x[b, 0:2176]
            mk3 = np.concatenate([mk_lo, mk_int, mk_int], axis=0)
        else:
            xc[0:2176] = x[b, 1920:4096]
            mk3 = np.concatenate([mk_int, mk_int, mk_hi], axis=0)
        m = dict(shared)
        m["x"] = xc
        m["mk3"] = np.ascontiguousarray(mk3)
        in_maps.append(m)
    return in_maps


_NC_CACHE = {}


def kernel(**inputs):
    in_maps = _host_inputs(**inputs)
    if "nc" not in _NC_CACHE:
        _NC_CACHE["nc"] = build_nc()
    nc = _NC_CACHE["nc"]
    res = run_bass_kernel_spmd(nc, in_maps, core_ids=list(range(8)))
    y = np.empty((4, 4096, D), dtype=np.float32)
    for core in range(8):
        b, half = core // 2, core % 2
        y[b, half * 2048:(half + 1) * 2048] = res.results[core]["y"]
    return y


# version 29
# speedup vs baseline: 1.0514x; 1.0086x over previous
import numpy as np
import ml_dtypes
from contextlib import ExitStack

import concourse.bass as bass
import concourse.mybir as mybir
from concourse.bass_utils import run_bass_kernel_spmd

F32 = mybir.dt.float32
BF16 = mybir.dt.bfloat16
AF = mybir.ActivationFunctionType
ALU = mybir.AluOpType
AX = mybir.AxisListType
BF = ml_dtypes.bfloat16

D = 2048
NTP = 18
NEG = -1.0e30
EPS = 1e-5
INV_D = 1.0 / 2048.0
QSCALE = float(128 ** -0.5)

ENGS = ("sync", "scalar", "vector", "tensor", "gpsimd")


class Sem:
    __slots__ = ("h", "n")

    def __init__(self, h):
        self.h = h
        self.n = 0


class Prog:

    def __init__(self, nc, selfsems=None):
        self.nc = nc
        self.selfsems = selfsems or {}
        self.reset()

    def reset(self):
        self.ops = {e: [] for e in ENGS}
        self.lastwait = {e: {} for e in ENGS}
        self.lastinc = {}

    def op(self, eng, fn, inc=None, inc_n=1):
        ss = self.selfsems.get(eng)
        if ss is not None:
            li = self.lastinc.get(eng)
            if li is not None:
                self.wait(eng, li[0], li[1])
            if inc is not None:
                self.ops[eng].append(("op", fn, (inc, inc_n), None))
                inc.n += inc_n
                self.lastinc[eng] = (inc, inc.n)
            else:
                self.ops[eng].append(("op", fn, (ss, 1), None))
                ss.n += 1
                self.lastinc[eng] = (ss, ss.n)
        else:
            self.ops[eng].append(("op", fn, (inc, inc_n), None))
            if inc is not None:
                inc.n += inc_n

    def wait(self, eng, sem, val=None):
        v = sem.n if val is None else val
        if v <= 0:
            return
        lw = self.lastwait[eng]
        if lw.get(id(sem), -1) >= v:
            return
        lw[id(sem)] = v
        self.ops[eng].append(("wait", sem, v, None))

    def flush(self):
        with self.nc.Block() as blk:
            for eng in ENGS:
                lst = self.ops[eng]
                if not lst:
                    continue

                def make(lst):
                    def run(e):
                        for kind, a, b, c in lst:
                            if kind == "wait":
                                e.wait_ge(a.h, b)
                            else:
                                ins = a(e)
                                inc, inc_n = b
                                if inc is not None:
                                    ins.then_inc(inc.h, inc_n)
                    return run

                getattr(blk, eng)(make(lst))
        self.reset()


def build_nc(cfg=None):
    cfg = cfg or {}
    stop_after = cfg.get("stop_after")
    nc = bass.Bass("TRN2", target_bir_lowering=False)

    dx = nc.dram_tensor("x", [NTP * 128, D], F32, kind="ExternalInput")
    dw1 = nc.dram_tensor("w1", [D, D], F32, kind="ExternalInput")
    dwq = nc.dram_tensor("wq", [D, D], F32, kind="ExternalInput")
    dwk = nc.dram_tensor("wk", [D, D], F32, kind="ExternalInput")
    dwv = nc.dram_tensor("wv", [D, D], F32, kind="ExternalInput")
    dwo = nc.dram_tensor("wo", [D, D], F32, kind="ExternalInput")
    dvf = nc.dram_tensor("vflat", [D, 512], F32, kind="ExternalInput")
    duf = nc.dram_tensor("uflat", [512, D], F32, kind="ExternalInput")
    dw2 = nc.dram_tensor("w2p", [128, 128], F32, kind="ExternalInput")
    db1 = nc.dram_tensor("b1t", [128, 16], F32, kind="ExternalInput")
    db2 = nc.dram_tensor("b2b", [128, 8], F32, kind="ExternalInput")
    dsel = nc.dram_tensor("selb", [8, 512], BF16, kind="ExternalInput")
    didb = nc.dram_tensor("d_idb", [128, 128], BF16, kind="ExternalInput")
    dmq = nc.dram_tensor("d_mq", [4, 128], BF16, kind="ExternalInput")
    dmk = nc.dram_tensor("mk3", [3 * 4, 384], BF16, kind="ExternalInput")
    dy = nc.dram_tensor("y", [16 * 128, D], F32, kind="ExternalOutput")
    dtap = None
    if stop_after:
        dtap = nc.dram_tensor("tap", [10 * 128, D], F32, kind="ExternalOutput")

    ctx = ExitStack()
    sb = lambda n, s, d: ctx.enter_context(nc.sbuf_tensor(n, s, d))
    psb = lambda n, s, d: ctx.enter_context(nc.psum_tensor(n, s, d))

    Xb = sb("Xb", [128, 10 * D], F32)
    XnT = sb("XnT", [128, 10 * D], BF16)
    Xn = [sb("Xn0", [128, D], BF16), sb("Xn1", [128, D], BF16)]
    wS = [sb("wSA", [128, 8192], BF16), sb("wSB", [128, 8192], BF16)]
    W2b = sb("W2b", [128, 128], BF16)
    b1T = sb("b1T", [128, 16], F32)
    B2b = sb("B2b", [128, 8], F32)
    Selb = sb("Selb", [8, 512], BF16)
    idb = sb("idb", [128, 128], BF16)
    Mqb = sb("Mqb", [68, 128], BF16)
    Mqs = [Mqb[0:4, :], Mqb[32:36, :], Mqb[64:68, :]]
    Mkb = sb("Mkb", [68, 384], BF16)
    Mk = [Mkb[0:4, :], Mkb[32:36, :], Mkb[64:68, :]]
    stats = sb("stats", [128, 6 * 10], F32)
    lgacc = sb("lgacc", [128, 8 * 10], F32)
    alphB = sb("alphB", [128, 8 * 10], BF16)
    asf = [sb("asf0", [128, 8], F32), sb("asf1", [128, 8], F32)]
    asc = sb("asc", [128, 4 * 10], F32)

    pg = [psb("pg0", [128, 512], F32), psb("pg1", [128, 512], F32)]
    pg2s = psb("pg2s", [128, 512], F32)

    sems = {}

    def S(name):
        if name not in sems:
            sems[name] = Sem(ctx.enter_context(nc.semaphore(name)))
        return sems[name]

    p = Prog(nc)
    if cfg.get("selfsem", True):
        p.selfsems = {"vector": S("vself"), "scalar": S("sself")}

    XS = lambda l: Xb[:, l * D:(l + 1) * D]
    XnTS = lambda l, c: XnT[:, l * D + c * 128: l * D + (c + 1) * 128]

    def xnt_rhs(l0, ntile, c):
        pstr = XnT.ap().ap[0][0]
        return bass.AP(XnT, l0 * D + c * 128, [[pstr, 128], [D, ntile], [1, 128]])

    def slot_dst(slot, nchunk, width, base=0):
        pstr = slot.ap().ap[0][0]
        return bass.AP(slot, base, [[pstr, 128], [width, nchunk], [1, width]])

    def wpiece_cols(dram, col0, width=512):
        ncols = dram.shape[1]
        return bass.AP(dram, col0, [[ncols, 128], [128 * ncols, 16], [1, width]])

    def wpiece_cols_h(dram, col0, h, width=512):
        ncols = dram.shape[1]
        return bass.AP(dram, col0 + h * 8 * 128 * ncols,
                       [[ncols, 128], [128 * ncols, 8], [1, width]])

    def wpiece_rows(dram, row0):
        return bass.AP(dram, row0 * D, [[D, 128], [128 * D, 4], [1, D]])

    def wpiece_rows_h(dram, row0):
        return bass.AP(dram, row0 * D, [[D, 128], [128 * D, 2], [1, D]])

    s_dx, s_dy, s_dc = S("dx"), S("dy"), S("dc")
    s_dwS = {"A": S("dwA"), "B": S("dwB"), "C": S("dwC")}
    dxv = {}
    wdn = {}

    def dma_w(tag, slot_key, dst_ap, src_ap):
        sem = s_dwS[slot_key]
        p.op("gpsimd", lambda e, dd=dst_ap, ss=src_ap: e.dma_start(dd, ss),
             inc=sem, inc_n=16)
        wdn[tag] = (sem, sem.n)

    def wait_w(eng, tag):
        sem, v = wdn[tag]
        p.wait(eng, sem, v)


    cv = {}

    def ph_consts():
        pairs = [("b1t", b1T[:, :], db1[:, :]), ("b2b", B2b[:, :], db2[:, :]),
                 ("selb", Selb[:, :], dsel[:, :]), ("idb", idb[:, :], didb[:, :]),
                 ("mq0", Mqs[0], dmq[:, :]), ("mq1", Mqs[1], dmq[:, :]),
                 ("mq2", Mqs[2], dmq[:, :]),
                 ("mk0", Mk[0], dmk[0:4, :]), ("mk1", Mk[1], dmk[4:8, :]),
                 ("mk2", Mk[2], dmk[8:12, :])]
        for name, dst, src in pairs:
            p.op("sync", lambda e, dst=dst, src=src: e.dma_start(
                dst, src), inc=s_dc, inc_n=16)
            cv[name] = s_dc.n
        dma_w(("w2p",), "C", bass.AP(W2b, 0, [[W2b.ap().ap[0][0], 128], [1, 128]]),
              dw2[:, :])

    def ph_loadx(c):
        if c == 1:
            p.wait("sync", s_dy, 16 * 8)
        for l in range(10):
            g = 8 * c + l
            sem = S(f"dx{l}")
            p.op("sync", lambda e, l=l, g=g: e.dma_start(
                XS(l), dx[g * 128:(g + 1) * 128, :]), inc=sem, inc_n=16)
            dxv[(c, l)] = (sem, sem.n)

    def ln_stats(l, sk, gates, mavd):
        s_st, s_ma, s_gc = S("st"), S("ma"), S("gc")
        sl = lambda k: stats[:, l * 6 + k: l * 6 + k + 1]
        for (gs, gv) in gates:
            p.wait("scalar", gs, gv)
            p.wait("vector", gs, gv)
        p.op("vector", lambda e, l=l, a=sl(0): e.reduce_sum(
            a, XS(l), AX.X), inc=s_st)
        p.op("scalar", lambda e, l=l, sk=sk, a=sl(1): e.activation(
            sk, XS(l), AF.Square, accum_out=a), inc=s_st)
        p.wait("gpsimd", s_st)

        def gch(fn):
            p.op("gpsimd", fn, inc=s_gc)
            p.wait("gpsimd", s_gc)
        gch(lambda e, a=sl(2), b=sl(0): e.tensor_scalar_mul(a, b, INV_D))
        gch(lambda e, a=sl(3), b=sl(1): e.tensor_scalar_mul(a, b, INV_D))
        gch(lambda e, a=sl(0), b=sl(2): e.tensor_mul(a, b, b))
        gch(lambda e, a=sl(3), b=sl(0): e.tensor_sub(a, a, b))
        p.op("gpsimd", lambda e, a=sl(3): e.tensor_scalar_add(a, a, EPS), inc=s_ma)
        mavd[l] = s_ma.n

    def make_ln(c, pT, sink, wait_dx=False, premav=None):
        s_st, s_ma, s_rs, s_ap = S("st"), S("ma"), S("rs"), S("ap")
        s_tr, s_te = S("tr"), S("te")
        sl = lambda k, l: stats[:, l * 6 + k: l * 6 + k + 1]
        state = {"trb": [None, None], "te": [None] * len(pT), "pend": []}
        mav = dict(premav) if premav is not None else {}
        rsv = {}
        te = {}

        def stage1(l):
            gates = [dxv[(c, l)]] if wait_dx else []
            ln_stats(l, sink, gates, mav)

        def stage2(l):
            xn = Xn[l % 2]
            p.wait("scalar", s_ma, mav[l])
            p.op("scalar", lambda e, l=l: e.activation(
                sl(1, l), sl(3, l), AF.Sqrt), inc=s_rs)
            rsv[l] = s_rs.n
            p.wait("vector", s_rs, rsv[l])
            p.op("vector", lambda e, l=l: e.reciprocal(sl(4, l), sl(1, l)))
            if state["trb"][l % 2] is not None:
                p.wait("vector", s_tr, state["trb"][l % 2])
            p.op("vector", lambda e, l=l, xn=xn: e.tensor_scalar(
                xn[:, :], XS(l), sl(2, l), sl(4, l), ALU.subtract, ALU.mult),
                 inc=s_ap)
            p.wait("tensor", s_ap)
            for half in range(2):
                bi = half % len(pT)
                if state["te"][bi] is not None:
                    p.wait("tensor", s_te, state["te"][bi])
                for j in range(8):
                    cc = half * 8 + j
                    p.op("tensor", lambda e, xn=xn, cc=cc, bi=bi, j=j: e.transpose(
                        pT[bi][:, j * 128:(j + 1) * 128],
                        xn[:, cc * 128:(cc + 1) * 128], idb[:, :]),
                        inc=(s_tr if j == 7 else None))
                p.wait("vector", s_tr)
                p.op("vector", lambda e, l=l, half=half, bi=bi: e.tensor_copy(
                    XnT[:, l * D + half * 1024: l * D + (half + 1) * 1024],
                    pT[bi][:, :]), inc=s_te)
                state["te"][bi] = s_te.n
            state["trb"][l % 2] = s_tr.n
            te[l] = s_te.n

        def emit(l):
            if l in mav:
                stage2(l)
                return
            if state["pend"]:
                stage2(state["pend"].pop(0))
            stage1(l)
            state["pend"].append(l)

        def drain():
            while state["pend"]:
                stage2(state["pend"].pop(0))

        emit.te = te
        emit.sem = s_te
        emit.drain = drain
        return emit

    def ph_ffn_w1(c, pid, wb, tiles, h_s, plg, ln=None):
        s_hmm, s_hev, s_lgm, s_lge = S("hmm"), S("hev"), S("lgm"), S("lge")
        ob = 4096 - wb
        n = len(tiles)
        l0 = tiles[0]
        tsubs = [(l0 + 4 * i, min(4, n - 4 * i)) for i in range((n + 3) // 4)]
        p.wait("vector", s_dc)
        for l in tiles:
            p.op("vector", lambda e, l=l: e.tensor_copy(
                lgacc[:, l * 8:(l + 1) * 8], B2b[:, :]))
        wait_w("tensor", ("w2p",))
        hev_bank = [None, None]
        hev_end = {}

        def hsl(g, s4, a, b):
            off = (g % 2) * 5120 + s4 * 1280
            return h_s[:, off + a: off + b]

        def emit_logits(g, i, l):
            p.wait("tensor", s_lge)
            p.wait("tensor", s_hev, hev_end[g])
            for s4 in range(2):
                p.op("tensor", lambda e, l=l, i=i, s4=s4, g=g: e.matmul(
                    plg[:, l * 8:(l + 1) * 8],
                    hsl(g, s4, i * 128, (i + 1) * 128),
                    W2b[:, (g * 2 + s4) * 8:(g * 2 + s4 + 1) * 8],
                    start=(s4 == 0), stop=(s4 == 1)),
                    inc=(s_lgm if s4 == 1 else None))
            p.wait("vector", s_lgm)
            p.op("vector", lambda e, l=l: e.tensor_add(
                lgacc[:, l * 8:(l + 1) * 8], lgacc[:, l * 8:(l + 1) * 8],
                plg[:, l * 8:(l + 1) * 8]), inc=s_lge)

        if ln is not None:
            for l in tiles[:4]:
                ln(l)
        lnq = list(tiles[4:]) if ln is not None else []
        for g in range(8):
            slot = wS[g % 2]
            if g < 2:
                tag = ("w1", c, pid, g)
                if tag not in wdn:
                    dma_w(tag, "AB"[g % 2], slot_dst(slot, 16, 256, wb),
                          wpiece_cols(dw1, g * 256, 256))
                if g == 1:
                    for h in range(2):
                        dma_w(("vfh", c, pid, h), "AB"[h],
                              slot_dst(wS[h], 8, 512, ob),
                              bass.AP(dvf, h * 8 * 128 * 512,
                                      [[512, 128], [128 * 512, 8], [1, 512]]))
            wait_w("tensor", ("w1", c, pid, g))
            if g == 0:
                pieces = [(s4, t0, nt) for (t0, nt) in tsubs for s4 in range(2)]
            else:
                pieces = [(s4, t0, nt) for s4 in range(2) for (t0, nt) in tsubs]
            for pi, (s4, t0, nt) in enumerate(pieces):
                bank = pg[pi % 2]
                if ln is not None:
                    while any(l2 not in ln.te for l2 in range(t0, t0 + nt)):
                        if lnq:
                            ln(lnq.pop(0))
                        else:
                            ln.drain()
                    p.wait("tensor", ln.sem, max(ln.te[l2]
                                                 for l2 in range(t0, t0 + nt)))
                if hev_bank[pi % 2] is not None:
                    p.wait("tensor", s_hev, hev_bank[pi % 2])
                for cc in range(16):
                    p.op("tensor", lambda e, bank=bank, slot=slot, cc=cc, s4=s4,
                         t0=t0, nt=nt: e.matmul(
                             bank[:, 0:nt * 128],
                             slot[:, wb + cc * 256 + s4 * 128:
                                  wb + cc * 256 + (s4 + 1) * 128],
                             xnt_rhs(t0, nt, cc),
                             start=(cc == 0), stop=(cc == 15)),
                         inc=(s_hmm if cc == 15 else None))
                if g > 0 and pi < n:
                    emit_logits(g - 1, pi, tiles[pi])
                if g == 0 and lnq:
                    ln(lnq.pop(0))
                p.wait("scalar", s_hmm)
                toff = (t0 - l0) * 128
                p.op("scalar", lambda e, bank=bank, s4=s4, toff=toff, nt=nt, g=g:
                     e.activation(hsl(g, s4, toff, toff + nt * 128),
                                  bank[:, 0:nt * 128], AF.Relu,
                                  bias=b1T[:, g * 2 + s4: g * 2 + s4 + 1]),
                     inc=s_hev)
                hev_bank[pi % 2] = s_hev.n
            if g > 0:
                for i2 in range(len(pieces), n):
                    emit_logits(g - 1, i2, tiles[i2])
            hev_end[g] = s_hev.n
            if g + 2 < 8:
                p.wait("gpsimd", s_hmm)
                dma_w(("w1", c, pid, g + 2), "AB"[g % 2],
                      slot_dst(wS[(g + 2) % 2], 16, 256, wb),
                      wpiece_cols(dw1, (g + 2) * 256, 256))
            if g >= 6:
                p.wait("gpsimd", s_hmm)
                h = g - 6
                dma_w(("ufh", c, pid, h), "AB"[h], slot_dst(wS[h], 2, D, wb),
                      bass.AP(duf, h * 256 * D, [[D, 128], [128 * D, 2], [1, D]]))
        return [(lambda i=i, l=l: emit_logits(7, i, l))
                for i, l in enumerate(tiles)]

    uxv = {}

    def ph_ffn_vu(c, pid, wb, tiles, z4, Ab4, aT, pAT, tail=None, prefetch=None,
                  stats_cb=None):
        ob = 4096 - wb
        s_alm, s_ale, s_alx = S("alm"), S("ale"), S("alx")
        s_vm, s_ve, s_um, s_ux = S("vm"), S("ve"), S("um"), S("ux")
        s_atm, s_ate, s_abm, s_abe = S("atm"), S("ate"), S("abm"), S("abe")
        s_hev = S("hev")
        n = len(tiles)
        l0 = tiles[0]
        tails = list(tail or [])
        alxv = {}
        alx_after = {}

        def router(i, l):
            par = i % 2
            sl = lambda k, l: asc[:, l * 4 + k: l * 4 + k + 1]
            p.op("vector", lambda e, l=l: e.reduce_max(
                sl(0, l), lgacc[:, l * 8:(l + 1) * 8], AX.X, negate=True), inc=s_alm)
            if alx_after.get(par) is not None:
                p.wait("scalar", s_alx, alx_after[par])
            p.wait("scalar", s_alm)
            p.op("scalar", lambda e, l=l, par=par: e.activation(
                asf[par][:, :], lgacc[:, l * 8:(l + 1) * 8], AF.Exp,
                bias=sl(0, l), accum_out=sl(1, l)), inc=s_ale)
            p.wait("vector", s_ale)
            p.op("vector", lambda e, l=l: e.reciprocal(sl(2, l), sl(1, l)))
            p.op("vector", lambda e, l=l, par=par: e.tensor_scalar_mul(
                alphB[:, l * 8:(l + 1) * 8], asf[par][:, :], sl(2, l)), inc=s_alx)
            alx_after[par] = s_alx.n
            alxv[l] = s_alx.n

        def alpha_sel(i0, nt):
            p.wait("tensor", s_alx, max(alxv[tiles[i0 + tt]] for tt in range(nt)))
            p.wait("tensor", s_ate)
            for tt in range(nt):
                l = tiles[i0 + tt]
                p.op("tensor", lambda e, l=l, tt=tt: e.transpose(
                    pAT[:, tt * 128:(tt + 1) * 128],
                    alphB[:, l * 8:(l + 1) * 8], idb[:, :]),
                    inc=(s_atm if tt == nt - 1 else None))
            p.wait("vector", s_atm)
            p.wait("vector", s_abm)
            p.op("vector", lambda e, nt=nt: e.tensor_copy(
                aT[:, 0:nt * 128], pAT[:, 0:nt * 128]), inc=s_ate)
            p.wait("tensor", s_ate)
            for kc2 in range(4):
                p.wait("tensor", s_abe)
                p.op("tensor", lambda e, kc2=kc2, nt=nt: e.matmul(
                    pg2s[:, 0:nt * 128],
                    Selb[:, kc2 * 128:(kc2 + 1) * 128], aT[:, 0:nt * 128],
                    start=True, stop=True), inc=s_abm)
                p.wait("vector", s_abm)
                p.op("vector", lambda e, kc2=kc2, nt=nt: e.tensor_copy(
                    Ab4[:, kc2 * 512: kc2 * 512 + nt * 128], pg2s[:, 0:nt * 128]),
                    inc=s_abe)
            return s_abe.n

        tsubs = [(4 * i, min(4, n - 4 * i)) for i in range((n + 3) // 4)]
        p.wait("tensor", s_dc)
        wait_w("tensor", ("vfh", c, pid, 0))
        wait_w("tensor", ("vfh", c, pid, 1))
        hev0 = s_hev.n
        ve_bank = [None, None]
        ux_bank = [None, None]
        pf = list(prefetch or [])
        for ti, (i0, nt) in enumerate(tsubs):
            first = (ti == 0)
            vmv = {}
            st = {"abe": None}

            def vx_mm(kc, splice=False, i0=i0, nt=nt, first=first, ti=ti,
                      vmv=vmv):
                bank = pg[kc % 2]
                p.wait("tensor", s_ux)
                if ve_bank[kc % 2] is not None:
                    p.wait("tensor", s_ve, ve_bank[kc % 2])
                if first and kc < 2:
                    p.wait("tensor", s_hev, hev0)
                for cc in range(16):
                    half, ci = cc // 8, cc % 8
                    p.op("tensor", lambda e, bank=bank, kc=kc, half=half, ci=ci,
                         cc=cc: e.matmul(
                             bank[:, 0:nt * 128],
                             wS[half][:, ob + ci * 512 + kc * 128:
                                      ob + ci * 512 + (kc + 1) * 128],
                             xnt_rhs(l0 + i0, nt, cc),
                             start=(cc == 0), stop=(cc == 15)),
                         inc=(s_vm if cc == 15 else None))
                    if splice and cc % 3 == 2 and tails:
                        tails.pop(0)()
                vmv[kc] = s_vm.n
                if ti == len(tsubs) - 1 and kc == 3 and pf:
                    p.wait("gpsimd", s_vm)
                    while pf:
                        dma_w(*pf.pop(0))

            def z_evac(kc, nt=nt, vmv=vmv, st=st):
                bank = pg[kc % 2]
                p.wait("vector", s_vm, vmv[kc])
                p.wait("vector", s_abe, st["abe"])
                p.wait("vector", s_um)
                p.op("vector", lambda e, bank=bank, kc=kc: e.scalar_tensor_tensor(
                    z4[:, kc * 512: kc * 512 + nt * 128], bank[:, 0:nt * 128], 0.5,
                    Ab4[:, kc * 512: kc * 512 + nt * 128], ALU.mult, ALU.mult),
                    inc=s_ve)
                ve_bank[kc % 2] = s_ve.n

            vx_mm(0, splice=first)
            if first:
                vx_mm(1, splice=True)
                while tails:
                    tails.pop(0)()
                for i, l in enumerate(tiles):
                    router(i, l)
                st["abe"] = alpha_sel(i0, nt)
            else:
                st["abe"] = alpha_sel(i0, nt)
                vx_mm(1)
            z_evac(0)
            z_evac(1)
            vx_mm(2)
            z_evac(2)
            vx_mm(3)
            z_evac(3)
            wait_w("tensor", ("ufh", c, pid, 0))
            wait_w("tensor", ("ufh", c, pid, 1))
            p.wait("tensor", s_ve)
            for tt in range(nt):
                l = tiles[i0 + tt]
                for cg in range(4):
                    bank = pg[cg % 2]
                    if ux_bank[cg % 2] is not None:
                        p.wait("tensor", s_ux, ux_bank[cg % 2])
                    for kc in range(4):
                        p.op("tensor", lambda e, bank=bank, tt=tt, cg=cg, kc=kc:
                             e.matmul(bank[:, :],
                                      z4[:, kc * 512 + tt * 128: kc * 512 + (tt + 1) * 128],
                                      wS[kc // 2][:, wb + (kc % 2) * D + cg * 512:
                                                  wb + (kc % 2) * D + (cg + 1) * 512],
                                      start=(kc == 0), stop=(kc == 3)),
                             inc=(s_um if kc == 3 else None))
                    p.wait("vector", s_um)
                    p.op("vector", lambda e, bank=bank, l=l, cg=cg: e.tensor_add(
                        Xb[:, l * D + cg * 512: l * D + (cg + 1) * 512],
                        Xb[:, l * D + cg * 512: l * D + (cg + 1) * 512],
                        bank[:, :]), inc=s_ux)
                    ux_bank[cg % 2] = s_ux.n
                uxv[l] = s_ux.n
                if stats_cb is not None:
                    stats_cb(l)

    def ph_attn(c, hg, A, ln=None, next_a=None, stats_cb=None):
        (Kb, Vt, Qb, Pn, PTs, aoAll, att4, pS, pPT) = A
        s_pm, s_peA, s_peV = S("pm"), S("peA"), S("peV")
        s_sm, s_nm, s_px, s_pn, s_ptm, s_ptd, s_aom, s_aoev, s_wom, s_wx = (
            S("sm"), S("nm"), S("px"), S("pn"), S("ptm"), S("ptd"), S("aom"),
            S("aoev"), S("wom"), S("wx"))
        tsubs = [(0, 4), (4, 4), (8, 2)]
        tiles10 = list(range(10))
        for h in range(2):
            tag = ("wk", c, hg, h)
            if tag not in wdn:
                dma_w(tag, "AB"[h], slot_dst(wS[h], 8, 512, 4096),
                      wpiece_cols_h(dwk, hg * 512, h))
        for h in range(2):
            dma_w(("wq", c, hg, h), "AB"[h], slot_dst(wS[h], 8, 512, 0),
                  wpiece_cols_h(dwq, hg * 512, h))
        p.wait("tensor", s_dc)

        pm_bank = [None, None]
        pmix = [0]
        kpe = {}
        vpe = {}
        qpe = {}

        def proj_piece(kind, *args):
            b = pmix[0] % 2
            pmix[0] += 1
            bank = pg[b]
            if pm_bank[b] is not None:
                p.wait("tensor", pm_bank[b][0], pm_bank[b][1])
            if kind == "k":
                h, (i0, nt) = args
                wait_w("tensor", ("wk", c, hg, 0))
                wait_w("tensor", ("wk", c, hg, 1))
                for cc in range(16):
                    p.op("tensor", lambda e, bank=bank, h=h, cc=cc, i0=i0, nt=nt:
                         e.matmul(bank[:, 0:nt * 128],
                                  wS[cc // 8][:, 4096 + (cc % 8) * 512 + h * 128:
                                              4096 + (cc % 8) * 512 + (h + 1) * 128],
                                  xnt_rhs(i0, nt, cc),
                                  start=(cc == 0), stop=(cc == 15)),
                         inc=(s_pm if cc == 15 else None))
                p.wait("scalar", s_pm)
                p.op("scalar", lambda e, bank=bank, h=h, i0=i0, nt=nt: e.copy(
                    Kb[:, h * 1280 + i0 * 128: h * 1280 + (i0 + nt) * 128],
                    bank[:, 0:nt * 128]), inc=s_peA)
                kpe[(h, i0)] = s_peA.n
                pm_bank[b] = (s_peA, s_peA.n)
            elif kind == "v":
                l, = args
                wait_w("tensor", ("wv", c, hg, 0))
                wait_w("tensor", ("wv", c, hg, 1))
                for cc in range(16):
                    p.op("tensor", lambda e, bank=bank, l=l, cc=cc: e.matmul(
                        bank[:, :], XnTS(l, cc),
                        wS[cc // 8][:, 4096 + (cc % 8) * 512:
                                    4096 + (cc % 8 + 1) * 512],
                        start=(cc == 0), stop=(cc == 15)),
                        inc=(s_pm if cc == 15 else None))
                p.wait("scalar", s_pm)
                p.op("scalar", lambda e, bank=bank, l=l: e.copy(
                    Vt[:, l * 512:(l + 1) * 512], bank[:, :]), inc=s_peA)
                vpe[l] = s_peA.n
                pm_bank[b] = (s_peA, s_peA.n)
            else:
                h, (i0, nt) = args
                wait_w("tensor", ("wq", c, hg, 0))
                wait_w("tensor", ("wq", c, hg, 1))
                for cc in range(16):
                    p.op("tensor", lambda e, bank=bank, h=h, cc=cc, i0=i0, nt=nt:
                         e.matmul(bank[:, 0:nt * 128],
                                  wS[cc // 8][:, (cc % 8) * 512 + h * 128:
                                              (cc % 8) * 512 + (h + 1) * 128],
                                  xnt_rhs(i0, nt, cc),
                                  start=(cc == 0), stop=(cc == 15)),
                         inc=(s_pm if cc == 15 else None))
                p.wait("vector", s_pm)
                p.op("vector", lambda e, bank=bank, h=h, i0=i0, nt=nt:
                     e.tensor_scalar_mul(
                         Qb[:, h * 1024 + (i0 - 1) * 128: h * 1024 + (i0 - 1 + nt) * 128],
                         bank[:, 0:nt * 128], QSCALE), inc=s_peV)
                qpe[(h, i0)] = s_peV.n
                pm_bank[b] = (s_peV, s_peV.n)

        at = lambda k, par: att4[:, par * 8 + k: par * 8 + k + 1]
        at2 = lambda k, par: att4[:, par * 8 + k: par * 8 + k + 2]
        qtiles = list(cfg.get("qtiles", range(1, 9)))
        its = [(q, hp) for q in qtiles for hp in range(2)]
        n_it = len(its)
        smv = {}; nmv = {}; pxv = {}; pnv = {}; ptmv = {}; ptdv = {}
        aomv = {}; aoevv = {}
        wo_read = {}

        def st_a(i):
            q, hp = its[i]
            par = i % 2
            bank = pS[par]
            gq = 8 * c + q
            mi = 0 if gq == 1 else (2 if gq == 16 else 1)
            mkt, mqt = Mk[mi], Mqs[mi]
            qts = 1 if q <= 4 else 5
            p.wait("tensor", s_peV, qpe[(2 * hp + 1, qts)])
            p.wait("tensor", s_peA, kpe[(2 * hp + 1, 8)])
            if i - 2 >= 0:
                p.wait("tensor", s_px, pxv[i - 2])
            for hh in range(2):
                h = 2 * hp + hh
                p.op("tensor", lambda e, bank=bank, mkt=mkt, mqt=mqt, hh=hh:
                     e.matmul(bank[:, hh * 512: hh * 512 + 384], mqt, mkt,
                              start=True, stop=False))
                p.op("tensor", lambda e, bank=bank, h=h, q=q, hh=hh: e.matmul(
                    bank[:, hh * 512: hh * 512 + 384],
                    Qb[:, h * 1024 + (q - 1) * 128: h * 1024 + q * 128],
                    Kb[:, h * 1280 + (q - 1) * 128: h * 1280 + (q - 1) * 128 + 384],
                    start=False, stop=True), inc=(s_sm if hh == 1 else None))
            smv[i] = s_sm.n

        def st_b(i):
            par = i % 2
            bank = pS[par]
            p.wait("vector", s_sm, smv[i])
            if i - 2 >= 0:
                p.wait("vector", s_px, pxv[i - 2])
            p.op("vector", lambda e, bank=bank, par=par: e.reduce_max(
                at(0, par),
                bass.AP(bank.tensor if hasattr(bank, "tensor") else bank, 0,
                        [[bank.ap().ap[0][0], 128], [512, 2], [1, 384]]),
                AX.XY, negate=True), inc=s_nm)
            nmv[i] = s_nm.n
            p.wait("scalar", s_nm, nmv[i])
            if i - 2 >= 0:
                p.wait("scalar", s_ptm, ptmv[i - 2])
            for hh in range(2):
                p.op("scalar", lambda e, bank=bank, par=par, hh=hh: e.activation(
                    Pn[par][:, hh * 384:(hh + 1) * 384],
                    bank[:, hh * 512: hh * 512 + 384], AF.Exp, bias=at(0, par),
                    accum_out=at(1 + hh, par)), inc=(s_px if hh == 1 else None))
            pxv[i] = s_px.n
            p.wait("vector", s_px, pxv[i])
            p.op("vector", lambda e, par=par: e.reciprocal(at2(3, par), at2(1, par)))
            if i - 2 >= 0:
                p.wait("vector", s_ptm, ptmv[i - 2])
            for hh in range(2):
                p.op("vector", lambda e, par=par, hh=hh: e.tensor_scalar_mul(
                    Pn[par][:, hh * 384:(hh + 1) * 384],
                    Pn[par][:, hh * 384:(hh + 1) * 384], at(3 + hh, par)),
                    inc=(s_pn if hh == 1 else None))
            pnv[i] = s_pn.n

        def st_c1(i):
            q, hp = its[i]
            par = i % 2
            p.wait("tensor", s_pn, pnv[i])
            p.wait("tensor", s_px, pxv[i])
            if i - 1 >= 0:
                p.wait("tensor", s_ptd, ptdv[i - 1])
            elif ln is not None:
                p.wait("tensor", ln.sem)
            for kt in range(6):
                p.op("tensor", lambda e, par=par, kt=kt: e.transpose(
                    pPT[:, kt * 128:(kt + 1) * 128],
                    Pn[par][:, kt * 128:(kt + 1) * 128], idb[:, :]),
                    inc=(s_ptm if kt == 5 else None))
            ptmv[i] = s_ptm.n
            eng = "scalar" if i % 2 else "vector"
            p.wait(eng, s_ptm, ptmv[i])
            if i - 2 >= 0:
                p.wait(eng, s_aom, aomv[i - 2])
            if eng == "scalar":
                p.op(eng, lambda e, par=par: e.copy(PTs[par][:, :], pPT[:, 0:768]),
                     inc=s_ptd)
            else:
                p.op(eng, lambda e, par=par: e.tensor_copy(PTs[par][:, :], pPT[:, 0:768]),
                     inc=s_ptd)
            ptdv[i] = s_ptd.n

        def st_c2(i):
            q, hp = its[i]
            par = i % 2
            qi = qtiles.index(q)
            p.wait("tensor", s_ptd, ptdv[i])
            p.wait("tensor", s_peA, vpe[min(q + 1, 9)])
            if hp == 0 and qi - 1 >= 0:
                p.wait("tensor", s_aoev, aoevv[qtiles[qi - 1]])
            for j in range(6):
                hh, kt = j // 3, j % 3
                h = 2 * hp + hh
                p.op("tensor", lambda e, par=par, h=h, hh=hh, q=q, kt=kt: e.matmul(
                    pg2s[:, h * 128:(h + 1) * 128],
                    Vt[:, (q - 1 + kt) * 512 + h * 128:
                       (q - 1 + kt) * 512 + (h + 1) * 128],
                    pPT[:, :] if False else PTs[par][:, (hh * 3 + kt) * 128:(hh * 3 + kt + 1) * 128],
                    start=(kt == 0), stop=(kt == 2)),
                    inc=(s_aom if j == 5 else None))
            aomv[i] = s_aom.n

        def st_d(i):
            q, hp = its[i]
            qi = qtiles.index(q)
            slot = qi % 4
            p.wait("scalar", s_aom, aomv[i])
            if wo_read.get(slot) is not None:
                p.wait("scalar", s_wom, wo_read[slot])
            p.op("scalar", lambda e, slot=slot: e.copy(
                aoAll[:, slot * 512:(slot + 1) * 512], pg2s[:, :]), inc=s_aoev)
            aoevv[q] = s_aoev.n

        wx_bank = [None, None]

        def wo_section(qs):
            p.wait("tensor", s_peA)
            p.wait("tensor", s_peV)
            for q in qs:
                qi = qtiles.index(q)
                slot = qi % 4
                p.wait("tensor", s_aoev, aoevv[q])
                for cg in range(4):
                    bsel = cg % 2
                    wbank = pg[bsel]
                    if wx_bank[bsel] is not None:
                        p.wait("tensor", s_wx, wx_bank[bsel])
                    for h2 in range(4):
                        p.op("tensor", lambda e, wbank=wbank, slot=slot, h2=h2, cg=cg:
                             e.matmul(wbank[:, :],
                                      aoAll[:, slot * 512 + h2 * 128: slot * 512 + (h2 + 1) * 128],
                                      wS[h2 // 2][:, (h2 % 2) * D + cg * 512:
                                                  (h2 % 2) * D + (cg + 1) * 512],
                                      start=(h2 == 0), stop=(h2 == 3)),
                             inc=(s_wom if h2 == 3 else None))
                    wv = s_wom.n
                    p.wait("vector", s_wom, wv)
                    p.op("vector", lambda e, wbank=wbank, q=q, cg=cg: e.tensor_add(
                        Xb[:, q * D + cg * 512: q * D + (cg + 1) * 512],
                        Xb[:, q * D + cg * 512: q * D + (cg + 1) * 512],
                        wbank[:, :]), inc=s_wx)
                    wx_bank[bsel] = s_wx.n
                wo_read[slot] = s_wom.n
                if stats_cb is not None:
                    stats_cb(q, (s_wx, s_wx.n))

        def ln_wait(t0, nt):
            if ln is not None:
                if any(l2 not in ln.te for l2 in range(t0, t0 + nt)):
                    ln.drain()
                p.wait("tensor", ln.sem, max(ln.te[l2] for l2 in range(t0, t0 + nt)))

        if ln is not None:
            for l in tiles10[:4]:
                if l not in ln.te:
                    ln(l)
        lnq = [l for l in tiles10[4:] if l not in ln.te] if ln is not None else []
        korder = [("k", h, ts) for ts in tsubs for h in range(4)]
        for (kind, h, ts) in korder:
            ln_wait(*ts)
            proj_piece("k", h, ts)
            if lnq:
                ln(lnq.pop(0))
        pmK = s_pm.n
        p.wait("gpsimd", s_pm, pmK)
        for h in range(2):
            dma_w(("wv", c, hg, h), "AB"[h], slot_dst(wS[h], 8, 512, 4096),
                  wpiece_cols_h(dwv, hg * 512, h))
        if ln is not None:
            ln.drain()
            p.wait("tensor", ln.sem)
        for h in range(4):
            proj_piece("q", h, (1, 4))
            proj_piece("q", h, (5, 4))
        vq = list(range(10))
        for l in range(3):
            proj_piece("v", vq.pop(0))
        p.wait("gpsimd", s_pm)
        for h in range(2):
            dma_w(("wo", c, hg, h), "AB"[h], slot_dst(wS[h], 2, D, 0),
                  wpiece_rows_h(dwo, hg * 512 + h * 256))
        for i in range(n_it + 3):
            if i < n_it:
                st_a(i)
            if vq:
                proj_piece("v", vq.pop(0))
                if not vq and next_a is not None:
                    p.wait("gpsimd", s_pm)
                    for spec in next_a:
                        dma_w(*spec)
            if 0 <= i - 1 < n_it:
                st_b(i - 1)
            if 0 <= i - 2 < n_it:
                st_c1(i - 2)
            if 0 <= i - 3 < n_it:
                st_c2(i - 3)
                if its[i - 3][1] == 1:
                    st_d(i - 3)
                    qdone = qtiles.index(its[i - 3][0])
                    if qdone % 2 == 1:
                        wait_w("tensor", ("wo", c, hg, 0))
                        wait_w("tensor", ("wo", c, hg, 1))
                        wo_section(qtiles[qdone - 1:qdone + 1])
        if len(qtiles) % 2 == 1:
            wait_w("tensor", ("wo", c, hg, 0))
            wait_w("tensor", ("wo", c, hg, 1))
            wo_section(qtiles[-1:])

    def ph_store(c, load_next=False, stats_cb=None):
        s_ux = S("ux")
        dy0 = s_dy.n
        for q in range(1, 9):
            g = 8 * c + q
            p.wait("sync", s_ux, uxv[q])
            p.op("sync", lambda e, q=q, g=g: e.dma_start(
                dy[(g - 1) * 128:g * 128, :], XS(q)), inc=s_dy, inc_n=16)
        if load_next:
            if stats_cb is not None:
                stats_cb(0)
            pend = []
            for l in range(1, 9):
                g = 8 * (c + 1) + l
                p.wait("gpsimd", s_dy, dy0 + 16 * l)
                sem = S(f"dx{l}")
                p.op("gpsimd", lambda e, l=l, g=g: e.dma_start(
                    XS(l), dx[g * 128:(g + 1) * 128, :]), inc=sem, inc_n=16)
                dxv[(c + 1, l)] = (sem, sem.n)
                if stats_cb is not None and l <= 3:
                    pend.append(l)
                    if len(pend) >= 2:
                        stats_cb(pend.pop(0))
            while pend:
                stats_cb(pend.pop(0))

    def ph_tap(tiles):
        p.wait("sync", S("ux"))
        p.wait("sync", S("wx"))
        for l in tiles:
            p.op("sync", lambda e, l=l: e.dma_start(
                dtap[l * 128:(l + 1) * 128, :], XS(l)), inc=s_dy, inc_n=16)
        p.wait("sync", s_dy)

    nchunks = 1 if stop_after else 2
    uid = [0]

    def un(base):
        uid[0] += 1
        return f"{base}_{uid[0]}"

    ph_consts()
    ln1mav = {}
    for c in range(nchunks):
        ln2mav = {}
        ln3mav = {}
        if (c, 0) not in dxv:
            ph_loadx(c)
        with ExitStack() as fs:
            h_s = fs.enter_context(nc.sbuf_tensor(un("h_s"), [128, 2 * 4 * 1280], BF16))
            z4 = fs.enter_context(nc.sbuf_tensor(un("z4"), [128, 4 * 512], BF16))
            Ab4 = fs.enter_context(nc.sbuf_tensor(un("Ab4"), [128, 4 * 512], BF16))
            aT = fs.enter_context(nc.sbuf_tensor(un("aT"), [8, 512], BF16))
            plg = fs.enter_context(nc.psum_tensor(un("plg"), [128, 80], F32))
            pT0 = fs.enter_context(nc.psum_tensor(un("pT0"), [128, 1024], BF16))
            pT1 = fs.enter_context(nc.psum_tensor(un("pT1"), [128, 1024], BF16))
            ln1 = make_ln(c, [pT0, pT1], h_s[:, 2560:4608], wait_dx=True,
                          premav=(ln1mav if c > 0 else None))
            tail = ph_ffn_w1(c, 1, 0, list(range(10)), h_s, plg, ln=ln1)
            with nc.psum_tensor(un("pAT"), [8, 512], BF16) as pAT:
                pfx = None
                sc2 = None
                if not stop_after:
                    pfx = [(("wk", c, 0, h), "AB"[h],
                            slot_dst(wS[h], 8, 512, 4096),
                            wpiece_cols_h(dwk, 0, h)) for h in range(2)]
                    sc2 = (lambda l, hs=h_s: ln_stats(
                        l, hs[:, 2560:4608], [(S("ux"), uxv[l])], ln2mav))
                ph_ffn_vu(c, 1, 0, list(range(10)), z4, Ab4, aT, pAT, tail=tail,
                          prefetch=pfx, stats_cb=sc2)
                p.flush()
        if stop_after == "s1":
            ph_tap(list(range(10)))
            p.flush()
            break
        with ExitStack() as asx:
            Kb = asx.enter_context(nc.sbuf_tensor(un("Kb"), [128, 4 * 1280], BF16))
            Vt = asx.enter_context(nc.sbuf_tensor(un("Vt"), [128, 10 * 512], BF16))
            Qb = asx.enter_context(nc.sbuf_tensor(un("Qb"), [128, 4 * 1024], BF16))
            Pn = [asx.enter_context(nc.sbuf_tensor(un("Pn"), [128, 768], BF16))
                  for _ in range(2)]
            PTs = [asx.enter_context(nc.sbuf_tensor(un("PTs"), [128, 768], BF16))
                   for _ in range(2)]
            aoAll = asx.enter_context(nc.sbuf_tensor(un("aoAll"), [128, 4 * 512], BF16))
            att4 = asx.enter_context(nc.sbuf_tensor(un("att4"), [128, 16], F32))
            pS = [asx.enter_context(nc.psum_tensor(un("pS"), [128, 1024], F32))
                  for _ in range(2)]
            pPT = asx.enter_context(nc.psum_tensor(un("pPT"), [128, 1024], BF16))
            A = (Kb, Vt, Qb, Pn, PTs, aoAll, att4, pS, pPT)
            ln2 = make_ln(c, [pPT], Qb[:, 0:2048],
                          premav=(ln2mav if not stop_after else None))
            for hg in range(cfg.get("hgs", 4)):
                scb = None
                if hg == 3 and not stop_after:
                    na = [(("w1", c, 3, g), "AB"[g], slot_dst(wS[g], 16, 256, 4096),
                           wpiece_cols(dw1, g * 256, 256)) for g in range(2)]
                    scb = (lambda q, gate: ln_stats(q, XS(9), [gate], ln3mav))
                elif hg + 1 < cfg.get("hgs", 4):
                    na = [(("wk", c, hg + 1, h), "AB"[h],
                           slot_dst(wS[h], 8, 512, 4096),
                           wpiece_cols_h(dwk, (hg + 1) * 512, h)) for h in range(2)]
                else:
                    na = None
                ph_attn(c, hg, A, ln=(ln2 if hg == 0 else None), next_a=na,
                        stats_cb=scb)
                p.flush()
        if stop_after == "att":
            ph_tap(list(range(10)))
            p.flush()
            break
        with ExitStack() as fs:
            h_s = fs.enter_context(nc.sbuf_tensor(un("h_s"), [128, 2 * 4 * 1280], BF16))
            z4 = fs.enter_context(nc.sbuf_tensor(un("z4"), [128, 4 * 512], BF16))
            Ab4 = fs.enter_context(nc.sbuf_tensor(un("Ab4"), [128, 4 * 512], BF16))
            aT = fs.enter_context(nc.sbuf_tensor(un("aT"), [8, 512], BF16))
            plg = fs.enter_context(nc.psum_tensor(un("plg"), [128, 80], F32))
            pT0 = fs.enter_context(nc.psum_tensor(un("pT0"), [128, 1024], BF16))
            pT1 = fs.enter_context(nc.psum_tensor(un("pT1"), [128, 1024], BF16))
            ln3 = make_ln(c, [pT0, pT1], h_s[:, 2560:4608], premav=ln3mav)
            tail = ph_ffn_w1(c, 3, 4096, list(range(1, 9)), h_s, plg, ln=ln3)
            if c + 1 < nchunks:
                for l in (0, 9):
                    g = 8 * (c + 1) + l
                    sem = S(f"dx{l}")
                    p.op("gpsimd", lambda e, l=l, g=g: e.dma_start(
                        XS(l), dx[g * 128:(g + 1) * 128, :]), inc=sem, inc_n=16)
                    dxv[(c + 1, l)] = (sem, sem.n)
            with nc.psum_tensor(un("pAT"), [8, 512], BF16) as pAT:
                pfx = None
                sc1 = None
                if c + 1 < nchunks:
                    pfx = [(("w1", c + 1, 1, g), "AB"[g],
                            slot_dst(wS[g], 16, 256, 0),
                            wpiece_cols(dw1, g * 256, 256)) for g in range(2)]
                    sc1 = (lambda l, hs=h_s: ln_stats(
                        l, hs[:, 2560:4608], [dxv[(c + 1, l)]], ln1mav))
                ph_ffn_vu(c, 3, 4096, list(range(1, 9)), z4, Ab4, aT, pAT,
                          tail=tail, prefetch=pfx)
                ph_store(c, load_next=(c + 1 < nchunks), stats_cb=sc1)
                p.flush()
    if not stop_after:
        p.wait("sync", s_dy, 16 * 16)
        p.flush()

    ctx.close()
    return nc



def _build_masks():
    NEGf = np.float32(NEG)
    mq = np.zeros((4, 128), dtype=BF)
    mq[0, :64] = 1
    mq[1, 64:] = 1
    mq[2, :] = 1

    def mk(pad_lo=False, pad_hi=False):
        m = np.zeros((4, 384), dtype=np.float32)
        m[0, 320:] = NEGf
        m[1, :64] = NEGf
        if pad_lo:
            m[2, :128] = NEGf
        if pad_hi:
            m[2, 256:] = NEGf
        return m.astype(BF)

    return mq, mk(), mk(pad_lo=True), mk(pad_hi=True)


def _host_inputs(L0_attn_hidden_states, L0_attn_Wq, L0_attn_Wk, L0_attn_Wv,
                 L0_attn_Wo, L0_ffn_U, L0_ffn_V, L0_ffn_W1, L0_ffn_b1,
                 L0_ffn_W2, L0_ffn_b2):
    x = np.asarray(L0_attn_hidden_states, dtype=np.float32)
    w1 = np.ascontiguousarray(np.asarray(L0_ffn_W1, dtype=np.float32))
    wq = np.ascontiguousarray(np.asarray(L0_attn_Wq, dtype=np.float32))
    wk = np.ascontiguousarray(np.asarray(L0_attn_Wk, dtype=np.float32))
    wv = np.ascontiguousarray(np.asarray(L0_attn_Wv, dtype=np.float32))
    wo = np.ascontiguousarray(np.asarray(L0_attn_Wo, dtype=np.float32))
    U = np.asarray(L0_ffn_U, dtype=np.float32)
    V = np.asarray(L0_ffn_V, dtype=np.float32)
    vflat = np.ascontiguousarray(V.transpose(2, 0, 1).reshape(D, 512))
    uflat = np.ascontiguousarray(U.transpose(0, 2, 1).reshape(512, D))
    W2 = np.asarray(L0_ffn_W2, dtype=np.float32)
    w2p = np.ascontiguousarray(
        W2.reshape(16, 128, 8).transpose(1, 0, 2).reshape(128, 128))
    b1t = np.ascontiguousarray(np.asarray(L0_ffn_b1, np.float32).reshape(16, 128).T)
    b2b = np.ascontiguousarray(
        np.broadcast_to(np.asarray(L0_ffn_b2, np.float32), (128, 8)))
    sel = np.zeros((8, 512), dtype=BF)
    for kr in range(512):
        sel[kr // 64, kr] = 1
    idb = np.eye(128, dtype=BF)
    mq, mk_int, mk_lo, mk_hi = _build_masks()

    shared = dict(w1=w1, wq=wq, wk=wk, wv=wv, wo=wo, vflat=vflat, uflat=uflat,
                  w2p=w2p, b1t=b1t, b2b=b2b, selb=sel, d_idb=idb, d_mq=mq)
    in_maps = []
    for core in range(8):
        b, half = core // 2, core % 2
        xc = np.zeros((NTP * 128, D), dtype=np.float32)
        if half == 0:
            xc[128:] = x[b, 0:2176]
            mk3 = np.concatenate([mk_lo, mk_int, mk_int], axis=0)
        else:
            xc[0:2176] = x[b, 1920:4096]
            mk3 = np.concatenate([mk_int, mk_int, mk_hi], axis=0)
        m = dict(shared)
        m["x"] = xc
        m["mk3"] = np.ascontiguousarray(mk3)
        in_maps.append(m)
    return in_maps


_NC_CACHE = {}


def kernel(**inputs):
    in_maps = _host_inputs(**inputs)
    if "nc" not in _NC_CACHE:
        _NC_CACHE["nc"] = build_nc()
    nc = _NC_CACHE["nc"]
    res = run_bass_kernel_spmd(nc, in_maps, core_ids=list(range(8)))
    y = np.empty((4, 4096, D), dtype=np.float32)
    for core in range(8):
        b, half = core // 2, core % 2
        y[b, half * 2048:(half + 1) * 2048] = res.results[core]["y"]
    return y
